# version 11
# baseline (speedup 1.0000x reference)
import contextlib
import numpy as np
import concourse.bass as bass
import concourse.mybir as mybir
from concourse.bass_types import AP
from concourse.bass_utils import run_bass_kernel_spmd

F32 = mybir.dt.float32
BF16 = mybir.dt.bfloat16
U8 = mybir.dt.uint8
AF = mybir.ActivationFunctionType
ALU = mybir.AluOpType

P = 128
T = 2048
NCH = 16
CL = 128
D = 2048
NDT = 16
OUT_CH = 8224
HID = 5632
NHT = 44
NH = 32
HD = 64
EPS = 1e-5
RT = 1.0 / 2048.0
RS = float(1.0 / np.sqrt(np.float64(2048.0)))


def _g2col(hh):
    return (hh % 4) * 512 + (hh // 4) * 128
STAGE_LIMIT = 99
TILE_LIMIT = 9999
SIMPLE_EVAC = False
NO_SPILL = False
NO_OVERLAP = False
NO_CHAIN = False
SKIP_A = False
A_FULL = False
A_SIMPLE = False
A_NOIP = False
NO_NEED = False
Z_ONLY = False
NO_ACT_TAP = True
S2_CHUNKS = 9999
NO_SSD_MM = False


class Ctr:
    def __init__(self, sem):
        self.sem = sem
        self.n = 0

    def inc(self, inst, k=1):
        inst.then_inc(self.sem, k)
        self.n += k
        return self.n

    def mark(self):
        return (self, self.n)


_CHAIN_OPS = {
    "tensor_tensor", "tensor_scalar", "tensor_copy", "scalar_tensor_tensor",
    "memset", "activation", "tensor_single_scalar", "reciprocal",
    "tensor_reduce", "tensor_add", "tensor_sub", "tensor_mul", "memzero",
    "tensor_scalar_mul", "tensor_scalar_add", "select", "transpose",
}


class _CH:

    def __init__(self, eng, ctr):
        self._eng = eng
        self._ctr = ctr

    def __getattr__(self, name):
        real = getattr(self._eng, name)
        if name not in _CHAIN_OPS:
            return real

        def wrapped(*a, **k):
            if self._ctr.n > 0 and not NO_CHAIN:
                self._eng.wait_ge(self._ctr.sem, self._ctr.n)
            inst = real(*a, **k)
            self._ctr.inc(inst, 1)
            return inst
        return wrapped


def _finalize(block, ALLC):
    @block.sync
    def _(sy):
        for c in ALLC:
            if c.n > 0:
                sy.wait_ge(c.sem, c.n)


def build_nc():
    nc = bass.Bass("TRN2", target_bir_lowering=False)

    d_uT = nc.dram_tensor("uT", [D, T + 3], F32, kind="ExternalInput").ap()
    d_ur = nc.dram_tensor("ur", [T, D], F32, kind="ExternalInput").ap()
    d_wiT = nc.dram_tensor("wiT", [D, OUT_CH], F32, kind="ExternalInput").ap()
    d_cw = nc.dram_tensor("cw", [P, 48, 4], F32, kind="ExternalInput").ap()
    d_cb = nc.dram_tensor("cb", [P, 48], F32, kind="ExternalInput").ap()
    d_dpk = nc.dram_tensor("dpk", [P, 16], F32, kind="ExternalInput").ap()
    d_zbp = nc.dram_tensor("zbp", [P, 16], F32, kind="ExternalInput").ap()
    d_woT = nc.dram_tensor("woT", [D, D], F32, kind="ExternalInput").ap()
    d_wqg = nc.dram_tensor("wqg", [D, HID], U8, kind="ExternalInput").ap()
    d_wqu = nc.dram_tensor("wqu", [D, HID], U8, kind="ExternalInput").ap()
    d_wqd = nc.dram_tensor("wqd", [HID, D], U8, kind="ExternalInput").ap()
    d_sg = nc.dram_tensor("sg", [P, NHT], F32, kind="ExternalInput").ap()
    d_su = nc.dram_tensor("su", [P, NHT], F32, kind="ExternalInput").ap()
    d_sdv = nc.dram_tensor("sdv", [1, D], F32, kind="ExternalInput").ap()
    d_msk = nc.dram_tensor("msk", [P, 1], F32, kind="ExternalInput").ap()
    d_id = nc.dram_tensor("idm", [P, P], F32, kind="ExternalInput").ap()
    d_idb = nc.dram_tensor("idb", [P, P], BF16, kind="ExternalInput").ap()
    d_ln = nc.dram_tensor("lnm", [P, P], F32, kind="ExternalInput").ap()
    d_ng = nc.dram_tensor("ngm", [P, P], F32, kind="ExternalInput").ap()
    d_on = nc.dram_tensor("onr", [1, P], F32, kind="ExternalInput").ap()

    d_xsp = nc.dram_tensor("xsp", [D, T], BF16).ap()
    d_bsp = nc.dram_tensor("bsp", [D, T], BF16).ap()
    d_csp = nc.dram_tensor("csp", [D, T], BF16).ap()
    d_zsp = nc.dram_tensor("zsp", [D, T], BF16).ap()
    d_stsp = nc.dram_tensor("stsp", [P, NCH * 1024], F32).ap()
    d_cum = nc.dram_tensor("cumh", [NH, T], F32).ap()
    d_x1 = nc.dram_tensor("x1sp", [T, D], F32).ap()
    d_x1b = nc.dram_tensor("x1bsp", [T, D], BF16).ap()
    d_x1t = nc.dram_tensor("x1tsp", [D, T], BF16).ap()
    d_negc = nc.dram_tensor("negc", [1, NHT * 1024], BF16, kind="ExternalInput").ap()
    d_cumB = nc.dram_tensor("cumB", [NCH, NH, CL], F32).ap()
    d_cumL = nc.dram_tensor("cumL", [NCH, NH], F32).ap()
    d_gh = nc.dram_tensor("ghsp", [HID, T], BF16).ap()
    d_dn0 = nc.dram_tensor("dn0sp", [T, 1024], F32).ap()
    d_ccs = nc.dram_tensor("ccs", [P, 1024], F32).ap()
    d_ccg = nc.dram_tensor("ccg", [2 * P, 1024], F32).ap()
    d_out = nc.dram_tensor("out", [T, D], F32, kind="ExternalOutput").ap()

    es = contextlib.ExitStack()
    with es:
        sba = lambda name, shape, dt: es.enter_context(nc.sbuf_tensor(name, shape, dt))
        cw_sb = sba("cw_sb", [P, 48, 4], F32)
        cb_sb = sba("cb_sb", [P, 48], F32)
        dpk_sb = sba("dpk_sb", [P, 16], F32)
        zbp_sb = sba("zbp_sb", [P, 16], F32)
        sg_sb = sba("sg_sb", [P, NHT], F32)
        su_sb = sba("su_sb", [P, NHT], F32)
        msk_sb = sba("msk_sb", [P, 1], F32)
        id_sb = sba("id_sb", [P, P], F32)
        idb_sb = sba("idb_sb", [P, P], BF16)
        ln2_sb = sba("ln2_sb", [P, P], F32)
        ng_sb = sba("ng_sb", [P, P], F32)
        on_sb = sba("on_sb", [1, P], F32)
        cumtm = sba("cumtm", [P, NCH * NH], F32)
        prev = sba("prev", [P, 1024], F32)
        prevb = sba("prevb", [P, 1024], BF16)
        cd0 = sba("cd0", [P, 16], F32)
        cd1 = sba("cd1", [P, 16], F32)
        stat = sba("stat", [P, 8], F32)

        epsc = sba("epsc", [P, 1], F32)
        nc.const_aps.register(F32, EPS, epsc[:, 0:1])
        ps = es.enter_context(nc.psum_tensor("ps", [P, 4096], F32))
        cdv = [cd0, cd1]

        def mkc(name):
            return Ctr(es.enter_context(nc.semaphore(name)))

        c_pe, c_ve, c_ac, c_cc = mkc("c_pe"), mkc("c_ve"), mkc("c_ac"), mkc("c_cc")
        c_gp = mkc("c_gp")
        g_uT, g_wo, g_wd = mkc("g_uT"), mkc("g_wo"), mkc("g_wd")
        g_wi = [mkc("g_wi0"), mkc("g_wi1")]
        g_wg = [mkc("g_wg0"), mkc("g_wg1")]
        g_wu = [mkc("g_wu0"), mkc("g_wu1")]
        l_cv = [mkc("l_cv0"), mkc("l_cv1")]
        l_st, l_cum, l_ccs, l_x1 = mkc("l_st"), mkc("l_cum"), mkc("l_ccs"), mkc("l_x1")
        l_uc, l_stc, l_dn = mkc("l_uc"), mkc("l_stc"), mkc("l_dn")
        l_gh = [mkc("l_gh0"), mkc("l_gh1")]
        y_k, y_ag, y_x1t, y_sdv = mkc("y_k"), mkc("y_ag"), mkc("y_x1t"), mkc("y_sdv")
        y_l = [mkc("y_l0"), mkc("y_l1")]
        y_tm = [mkc("y_tm0"), mkc("y_tm1")]
        y_e = [mkc("y_e0"), mkc("y_e1")]
        y_gc = [mkc("y_gc0"), mkc("y_gc1")]
        y_xd, y_out = mkc("y_xd"), mkc("y_out")
        l_xt = mkc("l_xt")
        y_h = mkc("y_h")
        ALLC = [c_pe, c_ve, c_ac, c_cc, c_gp, g_uT, g_wo, g_wd, g_wi[0], g_wi[1],
                g_wg[0], g_wg[1], g_wu[0], g_wu[1], l_cv[0], l_cv[1], l_st,
                l_cum, l_ccs, l_x1, l_uc, l_stc, l_dn, l_gh[0], l_gh[1],
                y_k, y_ag, y_l[0], y_l[1], y_x1t, y_sdv, y_tm[0], y_tm[1], y_e[0], y_e[1],
                y_gc[0], y_gc[1], y_xd, y_out, l_xt, y_h]

        block = es.enter_context(nc.Block())

        def snap():
            return [(c, c.n) for c in ALLC]

        def wait_all(eng, snapshot):
            for c, n in snapshot:
                if n > 0:
                    eng.wait_ge(c.sem, n)

        def wait_mark(eng, mark):
            ctr, val = mark
            if val > 0:
                eng.wait_ge(ctr.sem, val)

        @block.sync
        def _(sy):
            for dst, src in [(cw_sb, d_cw), (cb_sb, d_cb), (dpk_sb, d_dpk),
                             (zbp_sb, d_zbp), (sg_sb, d_sg), (su_sb, d_su),
                             (msk_sb, d_msk), (id_sb, d_id), (idb_sb, d_idb),
                             (ln2_sb, d_ln),
                             (ng_sb, d_ng), (on_sb, d_on)]:
                ap_d = dst[:, :, :] if len(dst.shape) == 3 else dst[:, :]
                y_k.inc(sy.dma_start(out=ap_d, in_=src), 16)
        syn_const = y_k.n

        @block.vector
        def _(v):
            v = _CH(v, c_ve)
            v.memset(epsc[:, 0:1], EPS)

        st15 = contextlib.ExitStack()
        dt_sb = st15.enter_context(nc.sbuf_tensor("dt_sb", [NH, T], F32))
        cumT = st15.enter_context(nc.sbuf_tensor("cumT", [NH, T], F32))
        st1 = contextlib.ExitStack()
        uT_sb = st1.enter_context(nc.sbuf_tensor("uT_sb", [P, NDT, T + 3], BF16))
        wi0 = st1.enter_context(nc.sbuf_tensor("wi0", [P, NDT, P], BF16))
        wi1 = st1.enter_context(nc.sbuf_tensor("wi1", [P, NDT, P], BF16))
        cv0 = st1.enter_context(nc.sbuf_tensor("cv0", [P, T], BF16))
        cv1 = st1.enter_context(nc.sbuf_tensor("cv1", [P, T], BF16))
        tapA = st1.enter_context(nc.sbuf_tensor("tapA", [P, T], F32))
        tapB = st1.enter_context(nc.sbuf_tensor("tapB", [P, T], F32))
        if True:
            wi = [wi0, wi1]
            cv = [cv0, cv1]
            tiles = [] if SKIP_A else [("A", 64, None, None)]
            if Z_ONLY:
                tiles += [("z", i, d_zsp, None) for i in range(48, 64)]
            tiles += [("c", i, d_xsp, i) for i in range(16)]
            tiles += [("c", i, d_bsp, i) for i in range(16, 32)]
            tiles += [("c", i, d_csp, i) for i in range(32, 48)]
            tiles += [("z", i, d_zsp, None) for i in range(48, 64)]
            tiles = tiles[:TILE_LIMIT]
            NT = len(tiles)

            CCH = [(0, 506, 0), (506, 1018, 1), (1018, 1530, 2), (1530, 2048, 4)]
            ZCH = [(0, 512, 0), (512, 1024, 1), (1024, 1536, 2), (1536, 2048, 3)]
            NEED_AFTER_CONV = {0: 2, 1: 3, 2: 4, 3: 4, 4: 4}
            NEED_AFTER_Z = {0: 1, 1: 2, 2: 3, 3: 4, 4: 4}
            if NO_OVERLAP:
                CCH = [(a, b, 4) for (a, b, g) in CCH]
                ZCH = [(a, b, 3) for (a, b, g) in ZCH]
                NEED_AFTER_CONV = {k: 4 for k in range(5)}
                NEED_AFTER_Z = {k: 4 for k in range(5)}

            uT_src = d_uT.rearrange("(dt p) t -> p dt t", p=P)
            wiT_src = d_wiT.rearrange("(dt p) c -> p dt c", p=P)

            @block.gpsimd
            def _(g):
                g = _CH(g, c_gp)
                for q in range(4):
                    g_uT.inc(g.dma_start(out=uT_sb[:, 4 * q:4 * (q + 1), :],
                                         in_=uT_src[:, 4 * q:4 * (q + 1), :]), 16)
            gld_uT = g_uT.n

            wi_marks = []
            grp_marks = []
            ev_marks = []
            spl_marks = []
            for ti, (kind, idx, spd, cr) in enumerate(tiles):
                ncols = 32 if (kind == "A" and not A_FULL) else P

                @block.gpsimd
                def _(g, ti=ti, ncols=ncols, idx=idx, kind=kind):
                    g = _CH(g, c_gp)
                    if ti >= 2:
                        g.wait_ge(c_pe.sem, grp_marks[ti - 2][-1])
                    col0 = idx * P if not (kind == "A" and A_FULL) else OUT_CH - P
                    g_wi[ti % 2].inc(g.dma_start(out=wi[ti % 2][:, :, 0:ncols],
                                                 in_=wiT_src[:, :, col0:col0 + ncols]), 16)
                    wi_marks.append(g_wi[ti % 2].mark())

                @block.tensor
                def _(t, ti=ti, kind=kind, ncols=ncols):
                    if ti == 0:
                        t.wait_ge(y_k.sem, syn_const)
                        t.wait_ge(g_uT.sem, gld_uT)
                    wait_mark(t, wi_marks[ti])
                    if kind in ("c", "A"):
                        groups = [(0, 512, 0), (512, 1024, 512), (1024, 1536, 1024),
                                  (1536, 2048, 1536), (2048, 2051, 2048)]
                    else:
                        groups = [(3, 515, 0), (515, 1027, 512), (1027, 1539, 1024),
                                  (1539, 2051, 1536)]
                    gm = []
                    prev_kind = tiles[ti - 1][0] if ti >= 1 else None
                    for gi, (a, b, pcol) in enumerate(groups):
                        if ti >= 1 and not NO_NEED:
                            need = (NEED_AFTER_CONV if prev_kind in ("c", "A")
                                    else NEED_AFTER_Z)[gi]
                            wait_mark(t, ev_marks[ti - 1][need - 1])
                        w = b - a
                        for dti in range(NDT):
                            inst = t.matmul(ps[0:ncols, pcol:pcol + w],
                                            wi[ti % 2][:, dti, 0:ncols],
                                            uT_sb[:, dti, a:b],
                                            start=(dti == 0), stop=(dti == NDT - 1))
                        c_pe.inc(inst, 1)
                        gm.append(c_pe.n)
                    if kind == "z":
                        gm.append(gm[-1])
                    grp_marks.append(gm)

                this_ev = []
                if SIMPLE_EVAC and kind == "c":
                    @block.vector
                    def _(v, ti=ti, cr=cr):
                        v = _CH(v, c_ve)
                        v.wait_ge(c_pe.sem, grp_marks[ti][4])
                        if ti >= 2:
                            wait_mark(v, spl_marks[ti - 2])
                        v.scalar_tensor_tensor(tapA[:, 0:T], ps[:, 3:3 + T],
                                               cw_sb[:, cr, 3:4], ps[:, 3:3 + T],
                                               ALU.mult, ALU.bypass)
                        v.scalar_tensor_tensor(tapB[:, 0:T], ps[:, 2:2 + T],
                                               cw_sb[:, cr, 2:3], tapA[:, 0:T],
                                               ALU.mult, ALU.add)
                        v.scalar_tensor_tensor(tapA[:, 0:T], ps[:, 1:1 + T],
                                               cw_sb[:, cr, 1:2], tapB[:, 0:T],
                                               ALU.mult, ALU.add)
                        v.tensor_scalar(tapB[:, 0:T], tapA[:, 0:T], cb_sb[:, cr:cr + 1],
                                        None, ALU.add)
                        v.scalar_tensor_tensor(cv[ti % 2][:, 0:T], ps[:, 0:T],
                                               cw_sb[:, cr, 0:1], tapB[:, 0:T],
                                               ALU.mult, ALU.add)
                    mkv = (c_ve, c_ve.n)
                    this_ev = [mkv, mkv, mkv, mkv]
                    ev_marks.append(this_ev)
                elif kind == "A":
                    @block.scalar
                    def _(s, ti=ti):
                        s = _CH(s, c_ac)
                        s.wait_ge(c_pe.sem, grp_marks[ti][4])
                        r0 = 96 if A_FULL else 0
                        if A_SIMPLE:
                            s.activation(tapA[0:32, 0:T], ps[r0:r0 + 32, 3:3 + T], AF.Relu)
                            inst = s.activation(dt_sb[0:32, :], ps[r0:r0 + 32, 3:3 + T], AF.Relu)
                        else:
                            s.activation(tapA[0:32, 0:T], ps[r0:r0 + 32, 3:3 + T], AF.Abs)
                            s.activation(tapA[0:32, 0:T], tapA[0:32, 0:T], AF.Exp, scale=-1.0)
                            s.activation(tapA[0:32, 0:T], tapA[0:32, 0:T], AF.Ln, bias=1.0)
                            inst = s.activation(dt_sb[0:32, :], ps[r0:r0 + 32, 3:3 + T], AF.Relu)
                    ac_A = c_ac.n

                    @block.vector
                    def _(v, ac_A=ac_A):
                        v = _CH(v, c_ve)
                        v.wait_ge(c_ac.sem, ac_A)
                        if A_NOIP:
                            inst = v.tensor_tensor(tapB[0:32, 0:T], dt_sb[0:32, :],
                                                   tapA[0:32, 0:T], ALU.add)
                        else:
                            inst = v.tensor_tensor(dt_sb[0:32, :], dt_sb[0:32, :],
                                                   tapA[0:32, 0:T], ALU.add)
                    for _k in range(4):
                        this_ev.append((c_ve, c_ve.n))
                    ev_marks.append(this_ev)
                    spl_marks.append((l_cv[0], 0))
                elif kind == "c":
                    for j, (a, b, gn) in enumerate(CCH):
                        if NO_ACT_TAP:
                            @block.vector
                            def _(v, ti=ti, j=j, a=a, b=b, gn=gn, cr=cr):
                                v = _CH(v, c_ve)
                                v.wait_ge(c_pe.sem, grp_marks[ti][gn])
                                if ti >= 1:
                                    wait_mark(v, ev_marks[ti - 1][j])
                                v.tensor_scalar(tapA[:, a:b], ps[:, a + 3:b + 3],
                                                cw_sb[:, cr, 3:4], cb_sb[:, cr:cr + 1],
                                                ALU.mult, ALU.add)
                        else:
                            @block.scalar
                            def _(s, ti=ti, j=j, a=a, b=b, gn=gn, cr=cr):
                                s = _CH(s, c_ac)
                                s.wait_ge(c_pe.sem, grp_marks[ti][gn])
                                if ti >= 1:
                                    wait_mark(s, ev_marks[ti - 1][j])
                                inst = s.activation(tapA[:, a:b], ps[:, a + 3:b + 3], AF.Identity,
                                                    bias=cb_sb[:, cr:cr + 1],
                                                    scale=cw_sb[:, cr, 3:4])
                        ac_j = c_ac.n

                        @block.vector
                        def _(v, ti=ti, j=j, a=a, b=b, cr=cr, ac_j=ac_j):
                            v = _CH(v, c_ve)
                            if not NO_ACT_TAP:
                                v.wait_ge(c_ac.sem, ac_j)
                            if j == 0 and ti >= 2:
                                wait_mark(v, spl_marks[ti - 2])
                            v.scalar_tensor_tensor(tapB[:, a:b], ps[:, a + 2:b + 2],
                                                   cw_sb[:, cr, 2:3], tapA[:, a:b],
                                                   ALU.mult, ALU.add)
                            v.scalar_tensor_tensor(tapA[:, a:b], ps[:, a + 1:b + 1],
                                                   cw_sb[:, cr, 1:2], tapB[:, a:b],
                                                   ALU.mult, ALU.add)
                            inst = v.scalar_tensor_tensor(cv[ti % 2][:, a:b], ps[:, a:b],
                                                          cw_sb[:, cr, 0:1], tapA[:, a:b],
                                                          ALU.mult, ALU.add)
                        this_ev.append((c_ve, c_ve.n))
                    ev_marks.append(this_ev)
                else:
                    zcol = idx - 48
                    for j, (a, b, gn) in enumerate(ZCH):
                        @block.scalar
                        def _(s, ti=ti, j=j, a=a, b=b, gn=gn, zcol=zcol):
                            s = _CH(s, c_ac)
                            s.wait_ge(c_pe.sem, grp_marks[ti][gn])
                            if j == 0 and ti >= 2:
                                wait_mark(s, spl_marks[ti - 2])
                            inst = s.activation(cv[ti % 2][:, a:b], ps[:, a:b], AF.Silu,
                                                bias=zbp_sb[:, zcol:zcol + 1])
                        this_ev.append((c_ac, c_ac.n))
                    ev_marks.append(this_ev)

                if kind != "A":
                    if ti % 2 == 0:
                        @block.scalar
                        def _(s, ti=ti, idx=idx, spd=spd):
                            s = _CH(s, c_ac)
                            wait_mark(s, ev_marks[ti][3])
                            ct0 = idx % 16
                            l_cv[0].inc(s.dma_start(
                                out=spd.rearrange("(ci p) (ct t) -> p ct ci t",
                                                  p=P, t=CL)[:, ct0, :, :],
                                in_=cv[0][:, :].rearrange("p (ci t) -> p ci t", ci=NCH)), 16)
                    else:
                        @block.sync
                        def _(sy, ti=ti, idx=idx, spd=spd):
                            wait_mark(sy, ev_marks[ti][3])
                            ct0 = idx % 16
                            l_cv[1].inc(sy.dma_start(
                                out=spd.rearrange("(ci p) (ct t) -> p ct ci t",
                                                  p=P, t=CL)[:, ct0, :, :],
                                in_=cv[1][:, :].rearrange("p (ci t) -> p ci t", ci=NCH)), 16)
                    spl_marks.append(l_cv[ti % 2].mark())

            s1_end = snap()

        if STAGE_LIMIT < 2:
            _finalize(block, ALLC)
            st1.close()
            st15.close()
            return nc

        st1.close()
        st15b = contextlib.ExitStack()
        dttm = st15b.enter_context(nc.sbuf_tensor("dttm", [P, NH], F32))
        sdt0 = st15b.enter_context(nc.sbuf_tensor("sdt0", [NH, CL], F32))
        sdtm = st15b.enter_context(nc.sbuf_tensor("sdtm", [P, NH], F32))
        xtm0 = st15b.enter_context(nc.sbuf_tensor("xtm0", [P, T], BF16))
        xtm1 = st15b.enter_context(nc.sbuf_tensor("xtm1", [P, T], BF16))
        btm0 = st15b.enter_context(nc.sbuf_tensor("btm0", [P, T], BF16))
        btm1 = st15b.enter_context(nc.sbuf_tensor("btm1", [P, T], BF16))
        xtm2 = st15b.enter_context(nc.sbuf_tensor("xtm2", [P, T], BF16))
        btm2 = st15b.enter_context(nc.sbuf_tensor("btm2", [P, T], BF16))
        xcm0 = st15b.enter_context(nc.sbuf_tensor("xcm0", [P, T], BF16))
        xcm1 = st15b.enter_context(nc.sbuf_tensor("xcm1", [P, T], BF16))
        xcm2 = st15b.enter_context(nc.sbuf_tensor("xcm2", [P, T], BF16))
        bcm0 = st15b.enter_context(nc.sbuf_tensor("bcm0", [P, T], BF16))
        bcm1 = st15b.enter_context(nc.sbuf_tensor("bcm1", [P, T], BF16))
        bcm2 = st15b.enter_context(nc.sbuf_tensor("bcm2", [P, T], BF16))
        bsd = st15b.enter_context(nc.sbuf_tensor("bsd", [P, T], BF16))
        stsb = st15b.enter_context(nc.sbuf_tensor("stsb", [P, 1024], F32))
        agp = st15b.enter_context(nc.sbuf_tensor("agp", [P, 1024], F32))
        dtm_all = st15b.enter_context(nc.sbuf_tensor("dtm_all", [P, NCH * NH], F32))
        clb = st15b.enter_context(nc.sbuf_tensor("clb", [P, NH], F32))
        sdb15 = st15b.enter_context(nc.sbuf_tensor("sdb15", [P, T], F32))
        if True:
            xtm = [xtm0, xtm1, xtm2]
            btm = [btm0, btm1, btm2]
            xcm15 = [xcm0, xcm1, xcm2]
            bcm15 = [bcm0, bcm1, bcm2]

            @block.vector
            def _(v):
                v = _CH(v, c_ve)
                wait_all(v, s1_end)
                v.memset(prev[:, :], 0.0)
                inst = v.tensor_copy(prevb[:, :], prev[:, :])

            @block.tensor
            def _(t):
                wait_all(t, s1_end)
                for ci in range(NCH):
                    inst = t.transpose(ps[:, 3584 + ci * NH:3584 + (ci + 1) * NH],
                                       dt_sb[:, ci * CL:(ci + 1) * CL], id_sb[0:32, 0:32])
                c_pe.inc(inst, 1)
            tr_pe = c_pe.n

            @block.vector
            def _(v):
                v = _CH(v, c_ve)
                v.wait_ge(c_pe.sem, tr_pe)
                inst = v.tensor_copy(dtm_all[:, :], ps[:, 3584:3584 + NCH * NH])
            dttm_ve = c_ve.n

            @block.tensor
            def _(t):
                t.wait_ge(c_ve.sem, dttm_ve)
                for ci in range(NCH):
                    t.matmul(ps[0:32, ci * CL:(ci + 1) * CL],
                             dtm_all[:, ci * NH:(ci + 1) * NH], ln2_sb[:, :],
                             start=True, stop=True)
                for ci in range(NCH):
                    inst = t.matmul(ps[:, 2048 + ci * NH:2048 + (ci + 1) * NH],
                                    ln2_sb[:, :], dtm_all[:, ci * NH:(ci + 1) * NH],
                                    start=True, stop=True)
                c_pe.inc(inst, 1)
            mm_pe = c_pe.n

            @block.vector
            def _(v):
                v = _CH(v, c_ve)
                v.wait_ge(c_pe.sem, mm_pe)
                v.tensor_copy(cumT[:, :], ps[0:32, 0:T])
                inst = v.tensor_copy(cumtm[:, :], ps[:, 2048:2048 + NCH * NH])
            cum_ve = c_ve.n

            @block.scalar
            def _(s):
                s = _CH(s, c_ac)
                s.wait_ge(c_ve.sem, cum_ve)
                l_cum.inc(s.dma_start(out=d_cum[:, :], in_=cumT[:, :]), 16)
                l_cum.inc(s.dma_start(
                    out=d_cumB.rearrange("ci h t -> h ci t"),
                    in_=cumT[:, :].rearrange("h (ci t) -> h ci t", ci=NCH)), 16)
                cta = cumT.ap()
                l_cum.inc(s.dma_start(
                    out=d_cumL.rearrange("ci h -> h ci"),
                    in_=AP(cta.tensor, cta.offset + CL - 1, [cta.ap[0], [CL, NCH]]),
                    allow_slow_non_contiguous=True), 16)
            scl_cum = l_cum.n
            cv_tot0, cv_tot1 = l_cv[0].n, l_cv[1].n

            cum_src = d_cum.rearrange("h t -> (h t)")
            cuml_src = d_cumL.rearrange("ci h -> (ci h)")
            carry_ve = 0
            st_pe = 0
            sd_ve = 0
            sd_ac = 0
            st_pe_hist = {}
            sd_ac_hist = {}
            tpcopy_hist = {}
            tmuse_hist = {}
            for ci in range(NCH):
                cs = ci * CL
                pf = ci % 3

                @block.sync
                def _(sy, ci=ci, cs=cs, pf=pf, st_pe=st_pe):
                    if ci == 0:
                        sy.wait_ge(l_cum.sem, scl_cum)
                        sy.wait_ge(l_cv[0].sem, cv_tot0)
                        sy.wait_ge(l_cv[1].sem, cv_tot1)
                    if ci >= 3:
                        sy.wait_ge(c_pe.sem, st_pe_hist[ci - 2])
                    if ci >= 1:
                        sy.wait_ge(c_ve.sem, sd_ve)
                    if ci >= 1:
                        sy.wait_ge(c_ac.sem, sd_ac_hist[ci - 1])
                    y_tm[ci % 2].inc(sy.dma_start(
                        out=xcm15[pf][:, :],
                        in_=d_xsp.rearrange("(ci p) c -> ci p c", p=P)[ci]), 16)
                    y_tm[ci % 2].inc(sy.dma_start(
                        out=bcm15[pf][:, :],
                        in_=d_bsp.rearrange("(ci p) c -> ci p c", p=P)[ci]), 16)
                    apin = AP(cuml_src.tensor, cuml_src.offset + ci * NH, [[0, P], [1, NH]])
                    y_tm[ci % 2].inc(sy.dma_start(out=clb[:, :], in_=apin), 16)
                strm_syn = y_tm[ci % 2].mark()

                @block.tensor
                def _(t, ci=ci, pf=pf, strm_syn=strm_syn):
                    wait_mark(t, strm_syn)
                    if ci >= 2:
                        t.wait_ge(c_ve.sem, tpcopy_hist[ci - 1])
                    pb = ps[:, 0:1024].bitcast(BF16)
                    pb2 = ps[:, 1024:2048].bitcast(BF16)
                    for ct in range(16):
                        t.transpose(pb[:, ct * CL:(ct + 1) * CL],
                                    xcm15[pf][:, ct * CL:(ct + 1) * CL], idb_sb[:, :])
                    for ct in range(16):
                        inst = t.transpose(pb2[:, ct * CL:(ct + 1) * CL],
                                           bcm15[pf][:, ct * CL:(ct + 1) * CL], idb_sb[:, :])
                    c_pe.inc(inst, 1)
                tp_pe = c_pe.n

                @block.vector
                def _(v, ci=ci, pf=pf, tp_pe=tp_pe):
                    v = _CH(v, c_ve)
                    v.wait_ge(c_pe.sem, tp_pe)
                    if ci >= 3:
                        wait_mark(v, tmuse_hist[ci - 2])
                    v.tensor_copy(xtm[pf][:, :], ps[:, 0:1024].bitcast(BF16))
                    inst = v.tensor_copy(btm[pf][:, :], ps[:, 1024:2048].bitcast(BF16))
                tpcopy = c_ve.n
                tpcopy_hist[ci] = tpcopy

                @block.vector
                def _(v, ci=ci, cs=cs, carry_ve=carry_ve, strm_syn=strm_syn, pf=pf):
                    v = _CH(v, c_ve)
                    if ci >= 1:
                        v.wait_ge(c_ve.sem, carry_ve)
                    wait_mark(v, strm_syn)
                    ca = clb.ap()
                    c3 = AP(ca.tensor, ca.offset, [ca.ap[0], [1, NH], [0, HD]])
                    ma = cumtm.ap()
                    m3 = AP(ma.tensor, ma.offset + ci * NH, [ma.ap[0], [1, NH], [0, HD]])
                    inst = v.tensor_tensor(
                        sdb15.ap().rearrange("p (h d) -> p h d", h=NH), c3, m3,
                        ALU.subtract)
                sd_ve = c_ve.n

                @block.scalar
                def _(s, ci=ci, pf=pf, sd_ve=sd_ve):
                    s = _CH(s, c_ac)
                    s.wait_ge(c_ve.sem, sd_ve)
                    s.activation(sdb15[:, :], sdb15[:, :], AF.Exp)
                    for blk in range(2):
                        pa = clb[blk * 64:(blk + 1) * 64, :]
                        inst = s.activation(cdv[ci % 2][blk * 64:(blk + 1) * 64, :],
                                            AP(pa.tensor, pa.offset + blk,
                                               [pa.ap[0], [2, 16]]), AF.Exp)
                sd_ac = c_ac.n
                sd_ac_hist[ci] = sd_ac

                @block.vector
                def _(v, pf=pf, sd_ac=sd_ac, tpcopy=tpcopy):
                    v = _CH(v, c_ve)
                    v.wait_ge(c_ac.sem, sd_ac)
                    v.wait_ge(c_ve.sem, tpcopy)
                    inst = v.tensor_tensor(bsd[:, :], btm[pf][:, :], sdb15[:, :], ALU.mult)
                bsd_ve = c_ve.n

                @block.tensor
                def _(t, pf=pf, bsd_ve=bsd_ve, ci=ci, carry_ve=carry_ve):
                    t.wait_ge(c_ve.sem, bsd_ve)
                    if ci >= 1:
                        t.wait_ge(c_ve.sem, carry_ve)
                    for h in range(NH):
                        rb = HD * (h % 2)
                        inst = t.matmul(ps[rb:rb + HD,
                                           2560 + (h // 2) * HD:2560 + (h // 2) * HD + HD],
                                        bsd[:, h * HD:(h + 1) * HD],
                                        xtm[pf][:, h * HD:(h + 1) * HD],
                                        start=True, stop=True, tile_position=(0, rb))
                    c_pe.inc(inst, 1)
                st_pe = c_pe.n
                st_pe_hist[ci] = st_pe
                tmuse_hist[ci] = (c_pe, st_pe)

                @block.vector
                def _(v, pf=pf, st_pe=st_pe, _lst=None):
                    v = _CH(v, c_ve)
                    v.wait_ge(c_pe.sem, st_pe)
                    if l_st.n > 0:
                        v.wait_ge(l_st.sem, l_st.n)
                    v.tensor_copy(stsb[:, :], ps[:, 2560:2560 + 1024])
                    cda = cdv[ci % 2].ap()
                    cd3 = AP(cda.tensor, cda.offset, [cda.ap[0], [1, 16], [0, HD]])
                    v.tensor_tensor(prev.ap().rearrange("p (r d) -> p r d", r=16),
                                    prev.ap().rearrange("p (r d) -> p r d", r=16),
                                    cd3, ALU.mult)
                    inst = v.tensor_tensor(prev[:, :], prev[:, :], stsb[:, :], ALU.add)
                carry_ve = c_ve.n

                @block.scalar
                def _(s, ci=ci, carry_ve=carry_ve):
                    s = _CH(s, c_ac)
                    s.wait_ge(c_ve.sem, carry_ve)
                    l_st.inc(s.dma_start(out=d_stsp[:, ci * 1024:(ci + 1) * 1024],
                                         in_=stsb[:, :]), 16)

            @block.scalar
            def _(s, carry_ve=carry_ve):
                s = _CH(s, c_ac)
                s.wait_ge(c_ve.sem, carry_ve)
                l_ccs.inc(s.dma_start(out=d_ccs[:, :], in_=prev[:, :]), 16)
            scl_ccs = l_ccs.n

            @block.gpsimd
            def _(g):
                g = _CH(g, c_gp)
                g.wait_ge(l_ccs.sem, scl_ccs)
                inst = g.collective_compute(
                    "AllGather", ALU.bypass,
                    replica_groups=[[0, 1], [2, 3], [4, 5], [6, 7]],
                    ins=[d_ccs.opt()], outs=[d_ccg.opt()])
                c_cc.inc(inst, 1)

            @block.sync
            def _(sy):
                sy.wait_ge(c_cc.sem, c_cc.n)
                y_ag.inc(sy.dma_start(out=agp[:, :], in_=d_ccg[0:P, :]), 16)
            syn_agp = y_ag.n

            @block.vector
            def _(v):
                v = _CH(v, c_ve)
                v.wait_ge(y_ag.sem, syn_agp)
                v.tensor_scalar(prev[:, :], agp[:, :], msk_sb[:, :], None, ALU.mult)
                inst = v.tensor_copy(prevb[:, :], prev[:, :])
            s15_end = snap()
        st15b.close()
        st15.close()
        if STAGE_LIMIT < 3:
            _finalize(block, ALLC)
            return nc

        st2 = contextlib.ExitStack()
        woT_sb = st2.enter_context(nc.sbuf_tensor("woT_sb", [P, NDT, D], BF16))
        xtc0 = st2.enter_context(nc.sbuf_tensor("xtc0", [P, T], BF16))
        xtc1 = st2.enter_context(nc.sbuf_tensor("xtc1", [P, T], BF16))
        btc0 = st2.enter_context(nc.sbuf_tensor("btc0", [P, T], BF16))
        btc1 = st2.enter_context(nc.sbuf_tensor("btc1", [P, T], BF16))
        ctc0 = st2.enter_context(nc.sbuf_tensor("ctc0", [P, T], BF16))
        ctc1 = st2.enter_context(nc.sbuf_tensor("ctc1", [P, T], BF16))
        szc0 = st2.enter_context(nc.sbuf_tensor("szc0", [P, T], BF16))
        szc1 = st2.enter_context(nc.sbuf_tensor("szc1", [P, T], BF16))
        xmc = st2.enter_context(nc.sbuf_tensor("xmc", [P, T], BF16))
        hbc = st2.enter_context(nc.sbuf_tensor("hbc", [P, 4096], F32))
        uc = st2.enter_context(nc.sbuf_tensor("uc", [P, D], F32))
        stc = st2.enter_context(nc.sbuf_tensor("stc", [P, 1024], F32))
        ebp = st2.enter_context(nc.sbuf_tensor("ebp", [P, T], BF16))
        cxp = st2.enter_context(nc.sbuf_tensor("cxp", [P, T], BF16))
        dif = st2.enter_context(nc.sbuf_tensor("dif", [P, T], F32))
        dcy = st2.enter_context(nc.sbuf_tensor("dcy", [P, T], BF16))
        g2m = st2.enter_context(nc.sbuf_tensor("g2m", [P, T], BF16))
        ytp = st2.enter_context(nc.sbuf_tensor("ytp", [P, D], BF16))
        ygt = st2.enter_context(nc.sbuf_tensor("ygt", [P, D], BF16))
        r1s = st2.enter_context(nc.sbuf_tensor("r1s", [P, D], F32))
        x1s = st2.enter_context(nc.sbuf_tensor("x1s", [P, D], F32))
        x1bs0 = st2.enter_context(nc.sbuf_tensor("x1bs0", [P, D], BF16))
        x1bs1 = st2.enter_context(nc.sbuf_tensor("x1bs1", [P, D], BF16))
        x1ts = st2.enter_context(nc.sbuf_tensor("x1ts", [P, D], BF16))
        if True:
            btc = [btc0, btc1]
            ctc = [ctc0, ctc1]
            xtc = [xtc0, xtc1]
            szc = [szc0, szc1]
            x1bs = [x1bs0, x1bs1]

            @block.gpsimd
            def _(g):
                g = _CH(g, c_gp)
                wait_all(g, s15_end)
                for q in range(4):
                    g_wo.inc(g.dma_start(
                        out=woT_sb[:, 4 * q:4 * (q + 1), :],
                        in_=d_woT.rearrange("(it p) o -> p it o", p=P)[:, 4 * q:4 * (q + 1), :]), 16)
            gld_wo = g_wo.n

            cum_src = d_cum.rearrange("h t -> (h t)")
            cumb_src = d_cumB.rearrange("ci h t -> (ci h t)")
            syn_early = {}
            syn_late = {}
            syn_hb = {}
            scl_uc = {}
            scl_st = {}

            def issue_early(sy, ci):
                pf = ci % 2
                y_e[pf].inc(sy.dma_start(out=btc[pf][:, :],
                                         in_=d_bsp.rearrange("(ci p) c -> ci p c", p=P)[ci]), 16)
                y_e[pf].inc(sy.dma_start(out=ctc[pf][:, :],
                                         in_=d_csp.rearrange("(ci p) c -> ci p c", p=P)[ci]), 16)
                syn_early[ci] = y_e[pf].mark()

            def issue_late(sy, ci):
                pf = ci % 2
                y_l[pf].inc(sy.dma_start(out=xtc[pf][:, :],
                                     in_=d_xsp.rearrange("(ci p) c -> ci p c", p=P)[ci]), 16)
                y_l[pf].inc(sy.dma_start(out=szc[pf][:, :],
                                     in_=d_zsp.rearrange("(ci p) c -> ci p c", p=P)[ci]), 16)
                syn_late[ci] = y_l[pf].mark()

            def issue_hb(sy, ci):
                apin = AP(cumb_src.tensor, cumb_src.offset + ci * NH * CL,
                          [[0, P], [1, NH * CL]])
                y_h.inc(sy.dma_start(out=hbc[:, :], in_=apin), 16)
                syn_hb[ci] = y_h.mark()

            @block.sync
            def _(sy):
                wait_all(sy, s15_end)
                issue_early(sy, 0)
                issue_late(sy, 0)
                issue_hb(sy, 0)
                issue_early(sy, 1)
                issue_late(sy, 1)

            @block.scalar
            def _(s):
                s = _CH(s, c_ac)
                wait_all(s, s15_end)
                cs = 0
                l_uc.inc(s.dma_start(out=uc[:, :], in_=d_ur[cs:cs + CL, :]), 16)
                scl_uc[0] = l_uc.n
                l_stc.inc(s.dma_start(out=stc[:, :], in_=d_stsp[:, 0:1024]), 16)
                scl_st[0] = l_stc.n

            prevb_ve = c_ve.n
            x1tr_pend = None
            ln_pend = None
            x1_ve = 0
            x1t_ve_m = [0]
            x1bs_free = {0: 0, 1: 0}
            ln_ac_hist = {}
            ebp_ac = 0
            cxp_ve = 0
            dif_ve = 0
            dcy_ac = 0
            g2_pe = 0
            g2m_ve = 0
            y_pe = 0
            gat_ve = 0
            wo_pe = 0
            wo_r1_ve = 0
            carry2_ve = 0
            carry_hist = {}
            chunk_ve = {}
            chunk_ac = {}

            def emit_ln(pci, p_wo_r1, x1_ve_prev):
                @block.scalar
                def _(s, p_wo_r1=p_wo_r1, x1_ve_prev=x1_ve_prev):
                    s = _CH(s, c_ac)
                    s.wait_ge(c_ve.sem, p_wo_r1)
                    if x1_ve_prev > 0:
                        s.wait_ge(c_ve.sem, x1_ve_prev)
                    s.activation(ytp[:, :], r1s[:, :], AF.Square, scale=RS,
                                 accum_out=stat[:, 0:1])
                    inst = s.activation(ytp[:, :], r1s[:, :], AF.Identity, scale=RT,
                                        accum_out=stat[:, 1:2])
                ln_ac = c_ac.n

                @block.vector
                def _(v, ln_ac=ln_ac):
                    v = _CH(v, c_ve)
                    v.wait_ge(c_ac.sem, ln_ac)
                    inst = v.scalar_tensor_tensor(stat[:, 2:3], stat[:, 1:2], stat[:, 1:2],
                                                  stat[:, 0:1], ALU.mult, ALU.subtract)
                nv_ve = c_ve.n

                @block.scalar
                def _(s, nv_ve=nv_ve):
                    s = _CH(s, c_ac)
                    s.wait_ge(c_ve.sem, nv_ve)
                    s.activation(stat[:, 3:4], stat[:, 2:3], AF.Ln, scale=-1.0, bias=EPS)
                    inst = s.activation(stat[:, 4:5], stat[:, 3:4], AF.Exp, scale=-0.5)
                rstd_ac = c_ac.n

                @block.vector
                def _(v, pci=pci, rstd_ac=rstd_ac):
                    v = _CH(v, c_ve)
                    v.wait_ge(c_ac.sem, rstd_ac)
                    if l_x1.n > 0:
                        v.wait_ge(l_x1.sem, l_x1.n)
                    if x1bs_free[pci % 2] > 0:
                        v.wait_ge(c_pe.sem, x1bs_free[pci % 2])
                    v.tensor_scalar(x1s[:, :], r1s[:, :], stat[:, 1:2], stat[:, 4:5],
                                    ALU.subtract, ALU.mult)
                    inst = v.tensor_copy(x1bs[pci % 2][:, :], x1s[:, :])
                x1_ve_new = c_ve.n

                @block.scalar
                def _(s, pci=pci, x1_ve_new=x1_ve_new):
                    s = _CH(s, c_ac)
                    s.wait_ge(c_ve.sem, x1_ve_new)
                    l_x1.inc(s.dma_start(out=d_x1[pci * CL:(pci + 1) * CL, :],
                                         in_=x1s[:, :]), 16)
                return x1_ve_new, ln_ac

            def consume_x1t(wo_r1_now):
                pci, px1ve = x1tr_pend[0], x1tr_pend[1]

                @block.tensor
                def _(t, pci=pci, px1ve=px1ve, wo_r1_now=wo_r1_now):
                    t.wait_ge(c_ve.sem, px1ve)
                    if wo_r1_now > 0:
                        t.wait_ge(c_ve.sem, wo_r1_now)
                    pb = ps[:, 1024:2048].bitcast(BF16)
                    for ct in range(16):
                        inst = t.transpose(pb[:, ct * CL:(ct + 1) * CL],
                                           x1bs[pci % 2][:, ct * CL:(ct + 1) * CL],
                                           idb_sb[:, :])
                    c_pe.inc(inst, 1)
                x1t_pe = c_pe.n
                x1bs_free[pci % 2] = x1t_pe

                @block.vector
                def _(v, x1t_pe=x1t_pe):
                    v = _CH(v, c_ve)
                    v.wait_ge(c_pe.sem, x1t_pe)
                    if l_xt.n > 0:
                        v.wait_ge(l_xt.sem, l_xt.n)
                    inst = v.tensor_copy(x1ts[:, :], ps[:, 1024:2048].bitcast(BF16))
                x1t_ve_m[0] = c_ve.n

                @block.scalar
                def _(s, pci=pci):
                    s = _CH(s, c_ac)
                    s.wait_ge(c_ve.sem, x1t_ve_m[0])
                    l_xt.inc(s.dma_start(
                        out=d_x1t.rearrange("(ci p) c -> ci p c", p=P)[pci],
                        in_=x1ts[:, :]), 16)

            for ci in range(min(NCH, S2_CHUNKS)):
                cs = ci * CL
                pf = ci % 2

                @block.scalar
                def _(s, ci=ci, pf=pf, cxp_ve=cxp_ve):
                    s = _CH(s, c_ac)
                    wait_mark(s, syn_hb[ci])
                    if ci >= 1:
                        s.wait_ge(c_ve.sem, cxp_ve)
                    if ci >= 2:
                        s.wait_ge(c_ve.sem, carry_hist[ci - 2])
                    for blk in range(2):
                        pa = hbc[blk * 64:(blk + 1) * 64, :]
                        s.activation(ebp[blk * 64:(blk + 1) * 64, :],
                                     AP(pa.tensor, pa.offset + blk * CL,
                                        [pa.ap[0], [2 * CL, 16], [1, CL]]), AF.Exp)
                    for blk in range(2):
                        pa = hbc[blk * 64:(blk + 1) * 64, :]
                        inst = s.activation(cdv[pf][blk * 64:(blk + 1) * 64, :],
                                            AP(pa.tensor, pa.offset + blk * CL + CL - 1,
                                               [pa.ap[0], [2 * CL, 16]]), AF.Exp)
                ebp_ac = c_ac.n

                @block.vector
                def _(v, ci=ci, pf=pf, ebp_ac=ebp_ac, y_pe=y_pe):
                    v = _CH(v, c_ve)
                    wait_mark(v, syn_early[ci])
                    v.wait_ge(c_ac.sem, ebp_ac)
                    if ci >= 1:
                        v.wait_ge(c_pe.sem, y_pe)
                    inst = v.tensor_tensor(cxp[:, :], ctc[pf][:, :], ebp[:, :], ALU.mult)
                cxp_ve = c_ve.n

                @block.tensor
                def _(t, ci=ci, g2m_ve=g2m_ve):
                    wait_mark(t, syn_late[ci])
                    if ci > 0:
                        t.wait_ge(c_ve.sem, g2m_ve)
                    pb = ps[:, 2048:3072].bitcast(BF16)
                    for ct in range(16):
                        inst = t.transpose(pb[:, ct * CL:(ct + 1) * CL],
                                           xtc[ci % 2][:, ct * CL:(ct + 1) * CL], idb_sb[:, :])
                    c_pe.inc(inst, 1)
                xm_pe = c_pe.n

                @block.vector
                def _(v, xm_pe=xm_pe, ci=ci, y_pe_prev=0):
                    v = _CH(v, c_ve)
                    v.wait_ge(c_pe.sem, xm_pe)
                    if ci > 0:
                        v.wait_ge(c_pe.sem, y_pe)
                    inst = v.tensor_copy(xmc[:, :], ps[:, 2048:3072].bitcast(BF16))
                xm_ve = c_ve.n

                for hf in range(2):
                    @block.vector
                    def _(v, ci=ci, hf=hf, dcy_ac=dcy_ac):
                        v = _CH(v, c_ve)
                        if ci > 0 or hf > 0:
                            v.wait_ge(c_ac.sem, dcy_ac)
                        if hf == 0:
                            wait_mark(v, syn_hb[ci])
                        da = dif.ap()
                        d4 = AP(da.tensor, da.offset, [da.ap[0], [512, 4], [CL, 4], [1, CL]])
                        ha = hbc.ap()
                        h4 = AP(ha.tensor, ha.offset + hf * 2048,
                                [ha.ap[0], [CL, 4], [512, 4], [1, CL]])
                        ca = cumtm.ap()
                        c4 = AP(ca.tensor, ca.offset + ci * NH + 16 * hf,
                                [ca.ap[0], [1, 4], [4, 4], [0, CL]])
                        na = ng_sb.ap()
                        n4 = AP(na.tensor, na.offset, [na.ap[0], [0, 4], [0, 4], [1, CL]])
                        v.tensor_tensor(d4, h4, c4, ALU.subtract)
                        inst = v.tensor_tensor(d4, d4, n4, ALU.add)
                    dif_ve = c_ve.n

                    @block.scalar
                    def _(s, dif_ve=dif_ve, g2m_ve=g2m_ve, ci=ci, hf=hf):
                        s = _CH(s, c_ac)
                        s.wait_ge(c_ve.sem, dif_ve)
                        if ci > 0 or hf > 0:
                            s.wait_ge(c_ve.sem, g2m_ve)
                        inst = s.activation(dcy[:, :], dif[:, :], AF.Exp)
                    dcy_ac = c_ac.n

                    @block.tensor
                    def _(t, ci=ci, pf=pf, hf=hf, g2m_ve=g2m_ve, xm_ve=xm_ve):
                        wait_mark(t, syn_early[ci])
                        if ci > 0 or hf > 0:
                            t.wait_ge(c_ve.sem, g2m_ve)
                        if hf == 0:
                            t.wait_ge(c_ve.sem, xm_ve)
                        if not NO_SSD_MM:
                            for hh in range(16):
                                h = 16 * hf + hh
                                rb = HD * (h % 2)
                                gc = 2048 + _g2col(hh)
                                inst = t.matmul(ps[:, gc:gc + CL],
                                                btc[pf][rb:rb + HD, (h // 2) * CL:(h // 2) * CL + CL],
                                                ctc[pf][rb:rb + HD, (h // 2) * CL:(h // 2) * CL + CL],
                                                start=True, stop=True, tile_position=(rb, 0))
                            c_pe.inc(inst, 1)
                        else:
                            inst = t.matmul(ps[:, 2048:2048 + CL], btc[pf][0:P, 0:CL],
                                            ctc[pf][0:P, 0:CL], start=True, stop=True)
                            c_pe.inc(inst, 1)
                    g2_pe = c_pe.n

                    if hf == 0 and x1tr_pend is not None:
                        consume_x1t(wo_r1_ve)
                        x1tr_pend = None

                    @block.vector
                    def _(v, g2_pe=g2_pe, dcy_ac=dcy_ac, y_pe=y_pe, ci=ci, hf=hf):
                        v = _CH(v, c_ve)
                        v.wait_ge(c_pe.sem, g2_pe)
                        v.wait_ge(c_ac.sem, dcy_ac)
                        if ci > 0 or hf > 0:
                            v.wait_ge(c_pe.sem, y_pe)
                        inst = v.tensor_tensor(g2m[:, :], ps[:, 2048:4096], dcy[:, :], ALU.mult)
                    g2m_ve = c_ve.n

                    @block.tensor
                    def _(t, ci=ci, hf=hf, g2m_ve=g2m_ve, cxp_ve=cxp_ve,
                          prevb_ve=prevb_ve, xm_ve=xm_ve, wo_r1_ve=wo_r1_ve):
                        t.wait_ge(c_ve.sem, g2m_ve)
                        t.wait_ge(c_ve.sem, cxp_ve)
                        t.wait_ge(c_ve.sem, prevb_ve)
                        t.wait_ge(c_ve.sem, xm_ve)
                        if hf == 0 and ci > 0:
                            t.wait_ge(c_ve.sem, wo_r1_ve)
                        if x1t_ve_m[0] > 0:
                            v_ = x1t_ve_m[0]
                            t.wait_ge(c_ve.sem, v_)
                        if not NO_SSD_MM:
                            for hh in range(16):
                                h = 16 * hf + hh
                                rb = HD * (h % 2)
                                pr = h // 2
                                t.matmul(ps[rb:rb + HD, pr * CL:pr * CL + CL],
                                         xmc[:, h * HD:(h + 1) * HD],
                                         g2m[:, _g2col(hh):_g2col(hh) + CL],
                                         start=True, stop=False, tile_position=(0, rb))
                                inst = t.matmul(ps[rb:rb + HD, pr * CL:pr * CL + CL],
                                                prevb[rb:rb + HD, pr * HD:pr * HD + HD],
                                                cxp[rb:rb + HD, pr * CL:pr * CL + CL],
                                                start=False, stop=True, tile_position=(rb, rb))
                            c_pe.inc(inst, 1)
                        else:
                            inst = t.matmul(ps[0:P, 0:CL], xmc[:, 0:P], g2m[:, 0:CL],
                                            start=True, stop=True)
                            c_pe.inc(inst, 1)
                    y_pe = c_pe.n

                if ci + 1 < min(NCH, S2_CHUNKS):
                    @block.sync
                    def _(sy, ci=ci, dif_ve=dif_ve, ebp_ac=ebp_ac):
                        sy.wait_ge(c_ve.sem, dif_ve)
                        sy.wait_ge(c_ac.sem, ebp_ac)
                        issue_hb(sy, ci + 1)

                if ln_pend is not None:
                    lpci, p_wo_r1 = ln_pend
                    x1_ve, _lnac = emit_ln(lpci, p_wo_r1, x1_ve)
                    ln_ac_hist[lpci] = _lnac
                    x1tr_pend = (lpci, x1_ve)
                    ln_pend = None

                @block.vector
                def _(v, ci=ci, y_pe=y_pe, wo_pe=wo_pe):
                    v = _CH(v, c_ve)
                    v.wait_ge(c_pe.sem, y_pe)
                    wait_mark(v, syn_late[ci])
                    if ci >= 1:
                        v.wait_ge(c_pe.sem, wo_pe)
                        v.wait_ge(c_ac.sem, ln_ac_hist[ci - 1])
                    xa = xtc[ci % 2].ap().rearrange("p (r q) -> p r q", r=16)
                    dpa = dpk_sb.ap()
                    dp3 = AP(dpa.tensor, dpa.offset, [dpa.ap[0], [1, 16], [0, CL]])
                    v.tensor_tensor(ytp.ap().rearrange("p (r q) -> p r q", r=16),
                                    xa, dp3, ALU.mult)
                    v.tensor_tensor(ytp[:, :], ytp[:, :], ps[:, 0:D], ALU.add)
                    inst = v.tensor_tensor(ygt[:, :], ytp[:, :], szc[ci % 2][:, :], ALU.mult)
                gat_ve = c_ve.n

                @block.vector
                def _(v, pf=pf, ebp_ac=ebp_ac, y_pe=y_pe, ci=ci):
                    v = _CH(v, c_ve)
                    v.wait_ge(c_ac.sem, ebp_ac)
                    v.wait_ge(c_pe.sem, y_pe)
                    v.wait_ge(l_stc.sem, scl_st[ci])
                    cda = cdv[pf].ap()
                    cd3 = AP(cda.tensor, cda.offset, [cda.ap[0], [1, 16], [0, HD]])
                    v.tensor_tensor(prev.ap().rearrange("p (r d) -> p r d", r=16),
                                    prev.ap().rearrange("p (r d) -> p r d", r=16),
                                    cd3, ALU.mult)
                    v.tensor_tensor(prev[:, :], prev[:, :], stc[:, :], ALU.add)
                    inst = v.tensor_copy(prevb[:, :], prev[:, :])
                carry2_ve = c_ve.n
                prevb_ve = carry2_ve
                carry_hist[ci] = carry2_ve

                if ci + 2 < min(NCH, S2_CHUNKS):
                    @block.sync
                    def _(sy, ci=ci, gat_ve=gat_ve, ebp_ac=ebp_ac):
                        sy.wait_ge(c_ve.sem, gat_ve)
                        sy.wait_ge(c_ac.sem, ebp_ac)
                        if ci >= 1:
                            sy.wait_ge(c_ve.sem, chunk_ve[ci - 1])
                            sy.wait_ge(c_ac.sem, chunk_ac[ci - 1])
                        issue_early(sy, ci + 2)
                        issue_late(sy, ci + 2)

                @block.tensor
                def _(t, ci=ci, gat_ve=gat_ve):
                    t.wait_ge(c_ve.sem, gat_ve)
                    if ci == 0:
                        t.wait_ge(g_wo.sem, gld_wo)
                    for og in range(4):
                        for it in range(NDT):
                            inst = t.matmul(ps[:, og * 512:(og + 1) * 512],
                                            ygt[:, it * CL:(it + 1) * CL],
                                            woT_sb[:, it, og * 512:(og + 1) * 512],
                                            start=(it == 0), stop=(it == NDT - 1))
                    c_pe.inc(inst, 1)
                wo_pe = c_pe.n

                @block.vector
                def _(v, ci=ci, wo_pe=wo_pe):
                    v = _CH(v, c_ve)
                    v.wait_ge(c_pe.sem, wo_pe)
                    v.wait_ge(l_uc.sem, scl_uc[ci])
                    if ci >= 1:
                        v.wait_ge(c_ac.sem, ln_ac_hist[ci - 1])
                    inst = v.tensor_tensor(r1s[:, :], ps[:, 0:D], uc[:, :], ALU.add)
                wo_r1_ve = c_ve.n

                if ci + 1 < min(NCH, S2_CHUNKS):
                    @block.scalar
                    def _(s, ci=ci, wo_r1_ve=wo_r1_ve):
                        s = _CH(s, c_ac)
                        s.wait_ge(c_ve.sem, wo_r1_ve)
                        l_uc.inc(s.dma_start(out=uc[:, :],
                                             in_=d_ur[(ci + 1) * CL:(ci + 2) * CL, :]), 16)
                        scl_uc[ci + 1] = l_uc.n

                ln_pend = (ci, wo_r1_ve)

                chunk_ve[ci] = c_ve.n
                chunk_ac[ci] = c_ac.n

                if ci + 1 < min(NCH, S2_CHUNKS):
                    @block.scalar
                    def _(s, ci=ci, carry2_ve=carry2_ve):
                        s = _CH(s, c_ac)
                        s.wait_ge(c_ve.sem, carry2_ve)
                        l_stc.inc(s.dma_start(out=stc[:, :],
                                              in_=d_stsp[:, (ci + 1) * 1024:(ci + 2) * 1024]), 16)
                        scl_st[ci + 1] = l_stc.n

            if x1tr_pend is not None:
                consume_x1t(wo_r1_ve)
                x1tr_pend = None
            if ln_pend is not None:
                lpci, p_wo_r1 = ln_pend
                x1_ve, _lnac = emit_ln(lpci, p_wo_r1, x1_ve)
                ln_ac_hist[lpci] = _lnac
                x1tr_pend = (lpci, x1_ve)
                ln_pend = None
            if x1tr_pend is not None:
                consume_x1t(wo_r1_ve)
                x1tr_pend = None

            s2_end = snap()

        if STAGE_LIMIT < 4:
            _finalize(block, ALLC)
            st2.close()
            return nc

        st2.close()
        st3a = contextlib.ExitStack()
        x1T = st3a.enter_context(nc.sbuf_tensor("x1T", [P, NDT, T], BF16))
        wg0 = st3a.enter_context(nc.sbuf_tensor("wg0", [P, NDT, P], BF16))
        wg1 = st3a.enter_context(nc.sbuf_tensor("wg1", [P, NDT, P], BF16))
        wu0 = st3a.enter_context(nc.sbuf_tensor("wu0", [P, NDT, P], BF16))
        wu1 = st3a.enter_context(nc.sbuf_tensor("wu1", [P, NDT, P], BF16))
        sgl = st3a.enter_context(nc.sbuf_tensor("sgl", [P, T], BF16))
        ght0 = st3a.enter_context(nc.sbuf_tensor("ght0", [P, T], BF16))
        ght1 = st3a.enter_context(nc.sbuf_tensor("ght1", [P, T], BF16))
        if True:
            wg = [wg0, wg1]
            wu = [wu0, wu1]
            ght = [ght0, ght1]

            @block.sync
            def _(sy):
                wait_all(sy, s2_end)
                y_x1t.inc(sy.dma_start(
                    out=x1T[:, :, :].rearrange("p dt (ci t) -> p dt ci t", t=CL),
                    in_=d_x1t.rearrange("(ci p) (dt t) -> p dt ci t", p=P, t=CL)), 16)
            syn_x1t = y_x1t.n

            wqg_src = d_wqg.rearrange("(dt p) h -> p dt h", p=P)
            wqu_src = d_wqu.rearrange("(dt p) h -> p dt h", p=P)

            mm_pe_m = {}
            ev_ve_m = {}
            sp_scl_m = {}
            gld_m = {}
            for ht in range(NHT):
                @block.gpsimd
                def _(g, ht=ht):
                    g = _CH(g, c_gp)
                    if ht == 0:
                        wait_all(g, s2_end)
                    if ht >= 2:
                        g.wait_ge(c_pe.sem, mm_pe_m[ht - 2])
                    g.memset(wg[ht % 2][:, :, :], -128.0)
                    g.memset(wu[ht % 2][:, :, :], -128.0)
                    g.wait_ge(c_gp.sem, c_gp.n)
                    g_wg[ht % 2].inc(g.dma_start(out=wg[ht % 2][:, :, :],
                                          in_=wqg_src[:, :, ht * P:(ht + 1) * P],
                                          accum_op=ALU.add), 16)
                    g_wu[ht % 2].inc(g.dma_start(out=wu[ht % 2][:, :, :],
                                          in_=wqu_src[:, :, ht * P:(ht + 1) * P],
                                          accum_op=ALU.add), 16)
                gld_m[ht] = (g_wg[ht % 2].mark(), g_wu[ht % 2].mark())

                @block.tensor
                def _(t, ht=ht):
                    if ht == 0:
                        t.wait_ge(y_x1t.sem, syn_x1t)
                    wait_mark(t, gld_m[ht][0])
                    wait_mark(t, gld_m[ht][1])
                    if ht >= 1:
                        t.wait_ge(c_ve.sem, ev_ve_m[ht - 1])
                    for tg in range(4):
                        for dti in range(NDT):
                            t.matmul(ps[:, tg * 512:(tg + 1) * 512],
                                     wg[ht % 2][:, dti, :],
                                     x1T[:, dti, tg * 512:(tg + 1) * 512],
                                     start=(dti == 0), stop=(dti == NDT - 1))
                        for dti in range(NDT):
                            inst = t.matmul(ps[:, 2048 + tg * 512:2048 + (tg + 1) * 512],
                                            wu[ht % 2][:, dti, :],
                                            x1T[:, dti, tg * 512:(tg + 1) * 512],
                                            start=(dti == 0), stop=(dti == NDT - 1))
                        c_pe.inc(inst, 1)
                mm_pe_m[ht] = c_pe.n

                @block.scalar
                def _(s, ht=ht):
                    s = _CH(s, c_ac)
                    for tg in range(4):
                        s.wait_ge(c_pe.sem, mm_pe_m[ht] - 3 + tg)
                        inst = s.activation(sgl[:, tg * 512:(tg + 1) * 512],
                                            ps[:, tg * 512:(tg + 1) * 512], AF.Silu,
                                            scale=sg_sb[:, ht:ht + 1])
                sil_ac = c_ac.n

                @block.vector
                def _(v, ht=ht, sil_ac=sil_ac):
                    v = _CH(v, c_ve)
                    if ht >= 2:
                        wait_mark(v, sp_scl_m[ht - 2])
                    for tg in range(4):
                        v.wait_ge(c_ac.sem, sil_ac - 3 + tg)
                        inst = v.scalar_tensor_tensor(
                            ght[ht % 2][:, tg * 512:(tg + 1) * 512],
                            ps[:, 2048 + tg * 512:2048 + (tg + 1) * 512],
                            su_sb[:, ht:ht + 1],
                            sgl[:, tg * 512:(tg + 1) * 512],
                            ALU.mult, ALU.mult)
                ev_ve_m[ht] = c_ve.n

                @block.scalar
                def _(s, ht=ht):
                    s = _CH(s, c_ac)
                    s.wait_ge(c_ve.sem, ev_ve_m[ht])
                    l_gh[ht % 2].inc(s.dma_start(out=d_gh[ht * P:(ht + 1) * P, :],
                                          in_=ght[ht % 2][:, :]), 16)
                sp_scl_m[ht] = l_gh[ht % 2].mark()

            s3a_end = snap()

        if STAGE_LIMIT < 5:
            _finalize(block, ALLC)
            st3a.close()
            return nc

        st3a.close()
        st3b = contextlib.ExitStack()
        wdh = st3b.enter_context(nc.sbuf_tensor("wdh", [P, NHT, 1024], BF16))
        ghc0 = st3b.enter_context(nc.sbuf_tensor("ghc0", [P, NHT, P], BF16))
        ghc1 = st3b.enter_context(nc.sbuf_tensor("ghc1", [P, NHT, P], BF16))
        sdb = st3b.enter_context(nc.sbuf_tensor("sdb", [P, D], F32))
        dnv = st3b.enter_context(nc.sbuf_tensor("dnv", [P, 1024], F32))
        x1ca = st3b.enter_context(nc.sbuf_tensor("x1ca", [P, D], F32))
        x1cb = st3b.enter_context(nc.sbuf_tensor("x1cb", [P, D], F32))
        dn0ca = st3b.enter_context(nc.sbuf_tensor("dn0ca", [P, 1024], F32))
        dn0cb = st3b.enter_context(nc.sbuf_tensor("dn0cb", [P, 1024], F32))
        r2s = st3b.enter_context(nc.sbuf_tensor("r2s", [P, D], F32))
        scr3 = st3b.enter_context(nc.sbuf_tensor("scr3", [P, D], F32))
        outsa = st3b.enter_context(nc.sbuf_tensor("outsa", [P, D], F32))
        outsb = st3b.enter_context(nc.sbuf_tensor("outsb", [P, D], F32))
        sdvs = st3b.enter_context(nc.sbuf_tensor("sdvs", [1, D], F32))
        if True:
            ghc = [ghc0, ghc1]
            x1c = [x1ca, x1cb]
            dn0c = [dn0ca, dn0cb]
            outs = [outsa, outsb]
            y_xd2 = [mkc("y_xd20"), mkc("y_xd21")]
            y_o2 = [mkc("y_o20"), mkc("y_o21")]
            for _c in y_xd2 + y_o2:
                ALLC.append(_c)
            wqd_src = d_wqd.rearrange("(ht p) o -> p ht o", p=P)
            gh_src = d_gh.rearrange("(ht p) t -> p ht t", p=P)

            @block.sync
            def _(sy):
                wait_all(sy, s3a_end)
                y_sdv.inc(sy.dma_start(out=sdvs[0:1, :], in_=d_sdv), 16)
            syn_sdv = y_sdv.n

            @block.tensor
            def _(t):
                t.wait_ge(y_sdv.sem, syn_sdv)
                wait_all(t, s3a_end)
                for og in range(4):
                    inst = t.matmul(ps[:, 2048 + og * 512:2048 + (og + 1) * 512],
                                    on_sb[0:1, :], sdvs[0:1, og * 512:(og + 1) * 512],
                                    start=True, stop=True)
                c_pe.inc(inst, 1)
            pe_sdb = c_pe.n

            @block.vector
            def _(v):
                v = _CH(v, c_ve)
                v.wait_ge(c_pe.sem, pe_sdb)
                wait_all(v, s3a_end)
                inst = v.tensor_copy(sdb[:, :], ps[:, 2048:4096])
            ve_sdb = c_ve.n

            mk = {}
            for oh in range(2):
                @block.gpsimd
                def _(g, oh=oh):
                    g = _CH(g, c_gp)
                    if oh == 0:
                        wait_all(g, s3a_end)
                    else:
                        g.wait_ge(c_pe.sem, mk[("half_pe", 0)])
                    negc_ap = AP(d_negc.tensor, d_negc.offset, [[0, P], [1, NHT * 1024]])
                    g_wd.inc(g.dma_start(out=wdh[:, :, :].rearrange("p h o -> p (h o)"),
                                         in_=negc_ap), 16)
                gld_f = g_wd.n

                @block.gpsimd
                def _(g, oh=oh, gld_f=gld_f):
                    g = _CH(g, c_gp)
                    g.wait_ge(g_wd.sem, gld_f)
                    g_wd.inc(g.dma_start(out=wdh[:, :, :],
                                          in_=wqd_src[:, :, oh * 1024:(oh + 1) * 1024],
                                          accum_op=ALU.add), 16)
                gld_wd = g_wd.n

                for tt in range(16):
                    pfb = tt % 2
                    ts0 = tt * CL
                    base = pfb * 1024

                    @block.sync
                    def _(sy, oh=oh, tt=tt, pfb=pfb, ts0=ts0):
                        if oh == 0 and tt == 0:
                            wait_all(sy, s3a_end)
                        gi = oh * 16 + tt
                        if gi >= 2:
                            po, ptt = divmod(gi - 2, 16)
                            sy.wait_ge(c_pe.sem, mk[("pe", po, ptt)])
                        y_gc[pfb].inc(sy.dma_start(out=ghc[pfb][:, :, :],
                                               in_=gh_src[:, :, ts0:ts0 + CL]), 16)
                        mk[("syn", oh, tt)] = y_gc[pfb].mark()
                        if oh == 1:
                            if tt >= 2:
                                sy.wait_ge(c_ve.sem, mk[("r2", tt - 2)])
                            if tt == 0:
                                sy.wait_ge(l_dn.sem, l_dn.n)
                                sy.wait_ge(l_x1.sem, l_x1.n)
                            y_xd2[pfb].inc(sy.dma_start(out=x1c[pfb][:, :],
                                                        in_=d_x1[ts0:ts0 + CL, :]), 16)
                            y_xd2[pfb].inc(sy.dma_start(out=dn0c[pfb][:, :],
                                                        in_=d_dn0[ts0:ts0 + CL, :]), 16)
                            mk[("xd", tt)] = y_xd2[pfb].mark()

                    @block.tensor
                    def _(t, oh=oh, tt=tt, pfb=pfb, base=base, gld_wd=gld_wd):
                        t.wait_ge(g_wd.sem, gld_wd)
                        wait_mark(t, mk[("syn", oh, tt)])
                        gi = oh * 16 + tt
                        if gi >= 2:
                            po, ptt = divmod(gi - 2, 16)
                            wait_mark(t, mk[("evd", po, ptt)])
                        for ht in range(NHT):
                            for og in range(2):
                                inst = t.matmul(ps[:, base + og * 512:base + (og + 1) * 512],
                                                ghc[pfb][:, ht, :],
                                                wdh[:, ht, og * 512:(og + 1) * 512],
                                                start=(ht == 0), stop=(ht == NHT - 1))
                        c_pe.inc(inst, 1)
                        mk[("pe", oh, tt)] = c_pe.n

                    if oh == 0:
                        @block.vector
                        def _(v, tt=tt, base=base):
                            v = _CH(v, c_ve)
                            v.wait_ge(c_pe.sem, mk[("pe", 0, tt)])
                            v.wait_ge(c_ve.sem, ve_sdb)
                            if tt >= 1:
                                v.wait_ge(l_dn.sem, mk[("dnscl", tt - 1)])
                            inst = v.tensor_tensor(dnv[:, :], ps[:, base:base + 1024],
                                                   sdb[:, 0:1024], ALU.mult)
                            mk[("evd", 0, tt)] = (c_ve, c_ve.n)

                        @block.scalar
                        def _(s, tt=tt, ts0=ts0):
                            s = _CH(s, c_ac)
                            wait_mark(s, mk[("evd", 0, tt)])
                            l_dn.inc(s.dma_start(out=d_dn0[ts0:ts0 + CL, :], in_=dnv[:, :]), 16)
                            mk[("dnscl", tt)] = l_dn.n
                    else:
                        @block.vector
                        def _(v, tt=tt, base=base, pfb=pfb):
                            v = _CH(v, c_ve)
                            v.wait_ge(c_pe.sem, mk[("pe", 1, tt)])
                            wait_mark(v, mk[("xd", tt)])
                            v.tensor_tensor(r2s[:, 1024:2048], ps[:, base:base + 1024],
                                            sdb[:, 1024:2048], ALU.mult)
                            v.tensor_tensor(r2s[:, 1024:2048], r2s[:, 1024:2048],
                                            x1c[pfb][:, 1024:2048], ALU.add)
                            inst = v.tensor_tensor(r2s[:, 0:1024], dn0c[pfb][:, :],
                                                   x1c[pfb][:, 0:1024], ALU.add)
                            mk[("evd", 1, tt)] = (c_ve, c_ve.n)
                            mk[("r2", tt)] = c_ve.n

                        @block.scalar
                        def _(s, tt=tt):
                            s = _CH(s, c_ac)
                            s.wait_ge(c_ve.sem, mk[("r2", tt)])
                            s.activation(scr3[:, :], r2s[:, :], AF.Square, scale=RS,
                                         accum_out=stat[:, 0:1])
                            inst = s.activation(scr3[:, :], r2s[:, :], AF.Identity, scale=RT,
                                                accum_out=stat[:, 1:2])
                            mk[("ac", tt)] = c_ac.n

                        @block.vector
                        def _(v, tt=tt):
                            v = _CH(v, c_ve)
                            v.wait_ge(c_ac.sem, mk[("ac", tt)])
                            inst = v.scalar_tensor_tensor(stat[:, 2:3], stat[:, 1:2],
                                                          stat[:, 1:2], stat[:, 0:1],
                                                          ALU.mult, ALU.subtract)
                            mk[("nv", tt)] = c_ve.n

                        @block.scalar
                        def _(s, tt=tt):
                            s = _CH(s, c_ac)
                            s.wait_ge(c_ve.sem, mk[("nv", tt)])
                            s.activation(stat[:, 3:4], stat[:, 2:3], AF.Ln, scale=-1.0, bias=EPS)
                            inst = s.activation(stat[:, 4:5], stat[:, 3:4], AF.Exp, scale=-0.5)
                            mk[("rstd", tt)] = c_ac.n

                        @block.vector
                        def _(v, tt=tt, pfb=pfb):
                            v = _CH(v, c_ve)
                            v.wait_ge(c_ac.sem, mk[("rstd", tt)])
                            if tt >= 2:
                                wait_mark(v, mk[("odma", tt - 2)])
                            inst = v.tensor_scalar(outs[pfb][:, :], r2s[:, :], stat[:, 1:2],
                                                   stat[:, 4:5], ALU.subtract, ALU.mult)
                            mk[("out", tt)] = c_ve.n

                        @block.gpsimd
                        def _(g, tt=tt, ts0=ts0, pfb=pfb):
                            g = _CH(g, c_gp)
                            g.wait_ge(c_ve.sem, mk[("out", tt)])
                            y_o2[pfb].inc(g.dma_start(out=d_out[ts0:ts0 + CL, :],
                                                      in_=outs[pfb][:, :]), 16)
                            mk[("odma", tt)] = y_o2[pfb].mark()

                mk[("half_pe", oh)] = c_pe.n

            @block.sync
            def _(sy):
                sy.wait_ge(y_o2[0].sem, y_o2[0].n)
                sy.wait_ge(y_o2[1].sem, y_o2[1].n)
                sy.wait_ge(l_dn.sem, l_dn.n)
        st3b.close()

    return nc


def kernel(L0_attn_u, L0_attn_W_in, L0_attn_conv_w, L0_attn_conv_b, L0_attn_D,
           L0_attn_z_bias, L0_attn_W_out, L0_ffn_wq_gate, L0_ffn_s_gate,
           L0_ffn_wq_up, L0_ffn_s_up, L0_ffn_wq_down, L0_ffn_s_down):
    u = np.asarray(L0_attn_u, dtype=np.float32)
    W_in = np.asarray(L0_attn_W_in, dtype=np.float32)
    conv_w = np.asarray(L0_attn_conv_w, dtype=np.float32)
    conv_b = np.asarray(L0_attn_conv_b, dtype=np.float32)
    Dv = np.asarray(L0_attn_D, dtype=np.float32)
    zb = np.asarray(L0_attn_z_bias, dtype=np.float32)
    W_out = np.asarray(L0_attn_W_out, dtype=np.float32)
    wqg = np.ascontiguousarray(np.asarray(L0_ffn_wq_gate, dtype=np.int32))
    sgv = np.asarray(L0_ffn_s_gate, dtype=np.float32)
    wqu = np.ascontiguousarray(np.asarray(L0_ffn_wq_up, dtype=np.int32))
    suv = np.asarray(L0_ffn_s_up, dtype=np.float32)
    wqd = np.ascontiguousarray(np.asarray(L0_ffn_wq_down, dtype=np.int32))
    sdw = np.asarray(L0_ffn_s_down, dtype=np.float32)

    def lowbyte(a):
        return a.view(np.uint8).reshape(a.shape[0], a.shape[1], 4)[:, :, 0]

    wiT = np.ascontiguousarray(W_in.T)
    woT = np.ascontiguousarray(W_out.T)
    wqgT = np.ascontiguousarray(lowbyte(wqg).T)
    wquT = np.ascontiguousarray(lowbyte(wqu).T)
    wqdT = np.ascontiguousarray(lowbyte(wqd).T)
    cw = np.ascontiguousarray(conv_w.reshape(48, P, 4).transpose(1, 0, 2))
    cb = np.ascontiguousarray(conv_b.reshape(48, P).T)
    dpk = np.ascontiguousarray(np.repeat(Dv, 64).reshape(16, P).T)
    zbp = np.ascontiguousarray(zb.reshape(16, P).T)
    sgp = np.ascontiguousarray(sgv.reshape(NHT, P).T)
    sup = np.ascontiguousarray(suv.reshape(NHT, P).T)
    sdv2 = np.ascontiguousarray(sdw.reshape(1, D))
    idm = np.eye(P, dtype=np.float32)
    import ml_dtypes
    idb = np.eye(P, dtype=ml_dtypes.bfloat16)
    k_ = np.arange(P)
    lnm = np.ascontiguousarray(-(k_[:, None] <= k_[None, :]).astype(np.float32))
    ngm = np.ascontiguousarray(
        np.where(k_[None, :] >= k_[:, None], 0.0, -60000.0).astype(np.float32))
    onr = np.ones((1, P), dtype=np.float32)
    negc = np.full((1, NHT * 1024), -128.0, dtype=ml_dtypes.bfloat16)

    in_maps = []
    for c in range(8):
        b, j = c // 2, c % 2
        t0 = j * T
        useg = np.ascontiguousarray(u[b, t0:t0 + T, :])
        if j == 0:
            uh = np.concatenate([np.zeros((3, D), np.float32), u[b, 0:T, :]], axis=0)
        else:
            uh = u[b, t0 - 3:t0 + T, :]
        uT = np.ascontiguousarray(uh.T)
        msk = np.full((P, 1), 1.0 if j == 1 else 0.0, np.float32)
        in_maps.append({
            "uT": uT, "ur": useg,
            "wiT": wiT, "cw": cw, "cb": cb, "dpk": dpk, "zbp": zbp,
            "woT": woT, "wqg": wqgT, "wqu": wquT, "wqd": wqdT,
            "sg": sgp, "su": sup, "sdv": sdv2, "msk": msk,
            "idm": idm, "idb": idb, "lnm": lnm, "ngm": ngm, "onr": onr,
            "negc": negc,
        })

    nc = build_nc()
    res = run_bass_kernel_spmd(nc, in_maps, core_ids=list(range(8)))
    out = np.empty((4, 4096, D), np.float32)
    for c in range(8):
        b, j = c // 2, c % 2
        out[b, j * T:(j + 1) * T, :] = res.results[c]["out"]
    return out


# version 12
# speedup vs baseline: 1.0172x; 1.0172x over previous
import contextlib
import numpy as np
import concourse.bass as bass
import concourse.mybir as mybir
from concourse.bass_types import AP
from concourse.bass_utils import run_bass_kernel_spmd

F32 = mybir.dt.float32
BF16 = mybir.dt.bfloat16
U8 = mybir.dt.uint8
AF = mybir.ActivationFunctionType
ALU = mybir.AluOpType

P = 128
T = 2048
NCH = 16
CL = 128
D = 2048
NDT = 16
OUT_CH = 8224
HID = 5632
NHT = 44
NH = 32
HD = 64
EPS = 1e-5
RT = 1.0 / 2048.0
RS = float(1.0 / np.sqrt(np.float64(2048.0)))


def _g2col(hh):
    return (hh % 4) * 512 + (hh // 4) * 128
STAGE_LIMIT = 99
TILE_LIMIT = 9999
SIMPLE_EVAC = False
NO_SPILL = False
NO_OVERLAP = False
NO_CHAIN = False
SKIP_A = False
A_FULL = False
A_SIMPLE = False
A_NOIP = False
NO_NEED = False
Z_ONLY = False
NO_ACT_TAP = True
S2_CHUNKS = 9999
NO_SSD_MM = False


class Ctr:
    def __init__(self, sem):
        self.sem = sem
        self.n = 0

    def inc(self, inst, k=1):
        inst.then_inc(self.sem, k)
        self.n += k
        return self.n

    def mark(self):
        return (self, self.n)


_CHAIN_OPS = {
    "tensor_tensor", "tensor_scalar", "tensor_copy", "scalar_tensor_tensor",
    "memset", "activation", "tensor_single_scalar", "reciprocal",
    "tensor_reduce", "tensor_add", "tensor_sub", "tensor_mul", "memzero",
    "tensor_scalar_mul", "tensor_scalar_add", "select", "transpose",
}


class _CH:

    def __init__(self, eng, ctr):
        self._eng = eng
        self._ctr = ctr

    def __getattr__(self, name):
        real = getattr(self._eng, name)
        if name not in _CHAIN_OPS:
            return real

        def wrapped(*a, **k):
            if self._ctr.n > 0 and not NO_CHAIN:
                self._eng.wait_ge(self._ctr.sem, self._ctr.n)
            inst = real(*a, **k)
            self._ctr.inc(inst, 1)
            return inst
        return wrapped


def _finalize(block, ALLC):
    @block.sync
    def _(sy):
        for c in ALLC:
            if c.n > 0:
                sy.wait_ge(c.sem, c.n)


def build_nc():
    nc = bass.Bass("TRN2", target_bir_lowering=False)

    d_uT = nc.dram_tensor("uT", [D, T + 3], F32, kind="ExternalInput").ap()
    d_ur = nc.dram_tensor("ur", [T, D], F32, kind="ExternalInput").ap()
    d_wiT = nc.dram_tensor("wiT", [D, OUT_CH], F32, kind="ExternalInput").ap()
    d_cw = nc.dram_tensor("cw", [P, 48, 4], F32, kind="ExternalInput").ap()
    d_cb = nc.dram_tensor("cb", [P, 48], F32, kind="ExternalInput").ap()
    d_dpk = nc.dram_tensor("dpk", [P, 16], F32, kind="ExternalInput").ap()
    d_zbp = nc.dram_tensor("zbp", [P, 16], F32, kind="ExternalInput").ap()
    d_woT = nc.dram_tensor("woT", [D, D], F32, kind="ExternalInput").ap()
    d_wqg = nc.dram_tensor("wqg", [D, HID], U8, kind="ExternalInput").ap()
    d_wqu = nc.dram_tensor("wqu", [D, HID], U8, kind="ExternalInput").ap()
    d_wqd = nc.dram_tensor("wqd", [HID, D], U8, kind="ExternalInput").ap()
    d_sg = nc.dram_tensor("sg", [P, NHT], F32, kind="ExternalInput").ap()
    d_su = nc.dram_tensor("su", [P, NHT], F32, kind="ExternalInput").ap()
    d_sdv = nc.dram_tensor("sdv", [1, D], F32, kind="ExternalInput").ap()
    d_msk = nc.dram_tensor("msk", [P, 1], F32, kind="ExternalInput").ap()
    d_id = nc.dram_tensor("idm", [P, P], F32, kind="ExternalInput").ap()
    d_idb = nc.dram_tensor("idb", [P, P], BF16, kind="ExternalInput").ap()
    d_ln = nc.dram_tensor("lnm", [P, P], F32, kind="ExternalInput").ap()
    d_ng = nc.dram_tensor("ngm", [P, P], F32, kind="ExternalInput").ap()
    d_on = nc.dram_tensor("onr", [1, P], F32, kind="ExternalInput").ap()

    d_xsp = nc.dram_tensor("xsp", [D, T], BF16).ap()
    d_bsp = nc.dram_tensor("bsp", [D, T], BF16).ap()
    d_csp = nc.dram_tensor("csp", [D, T], BF16).ap()
    d_zsp = nc.dram_tensor("zsp", [D, T], BF16).ap()
    d_stsp = nc.dram_tensor("stsp", [P, NCH * 1024], F32).ap()
    d_cum = nc.dram_tensor("cumh", [NH, T], F32).ap()
    d_x1 = nc.dram_tensor("x1sp", [T, D], F32).ap()
    d_x1b = nc.dram_tensor("x1bsp", [T, D], BF16).ap()
    d_x1t = nc.dram_tensor("x1tsp", [D, T], BF16).ap()
    d_negc = nc.dram_tensor("negc", [1, NHT * 1024], BF16, kind="ExternalInput").ap()
    d_cumB = nc.dram_tensor("cumB", [NCH, NH, CL], F32).ap()
    d_cumL = nc.dram_tensor("cumL", [NCH, NH], F32).ap()
    d_gh = nc.dram_tensor("ghsp", [HID, T], BF16).ap()
    d_dn0 = nc.dram_tensor("dn0sp", [T, 1024], F32).ap()
    d_ccs = nc.dram_tensor("ccs", [P, 1024], F32).ap()
    d_ccg = nc.dram_tensor("ccg", [2 * P, 1024], F32).ap()
    d_out = nc.dram_tensor("out", [T, D], F32, kind="ExternalOutput").ap()

    es = contextlib.ExitStack()
    with es:
        sba = lambda name, shape, dt: es.enter_context(nc.sbuf_tensor(name, shape, dt))
        cw_sb = sba("cw_sb", [P, 48, 4], F32)
        cb_sb = sba("cb_sb", [P, 48], F32)
        dpk_sb = sba("dpk_sb", [P, 16], F32)
        zbp_sb = sba("zbp_sb", [P, 16], F32)
        sg_sb = sba("sg_sb", [P, NHT], F32)
        su_sb = sba("su_sb", [P, NHT], F32)
        msk_sb = sba("msk_sb", [P, 1], F32)
        id_sb = sba("id_sb", [P, P], F32)
        idb_sb = sba("idb_sb", [P, P], BF16)
        ln2_sb = sba("ln2_sb", [P, P], F32)
        ng_sb = sba("ng_sb", [P, P], F32)
        on_sb = sba("on_sb", [1, P], F32)
        cumtm = sba("cumtm", [P, NCH * NH], F32)
        prev = sba("prev", [P, 1024], F32)
        prevb = sba("prevb", [P, 1024], BF16)
        cd0 = sba("cd0", [P, 16], F32)
        cd1 = sba("cd1", [P, 16], F32)
        stat = sba("stat", [P, 8], F32)

        epsc = sba("epsc", [P, 1], F32)
        nc.const_aps.register(F32, EPS, epsc[:, 0:1])
        ps = es.enter_context(nc.psum_tensor("ps", [P, 4096], F32))
        cdv = [cd0, cd1]

        def mkc(name):
            return Ctr(es.enter_context(nc.semaphore(name)))

        c_pe, c_ve, c_ac, c_cc = mkc("c_pe"), mkc("c_ve"), mkc("c_ac"), mkc("c_cc")
        c_gp = mkc("c_gp")
        g_uT, g_wo, g_wd = mkc("g_uT"), mkc("g_wo"), mkc("g_wd")
        g_wi = [mkc("g_wi0"), mkc("g_wi1")]
        g_wg = [mkc("g_wg0"), mkc("g_wg1")]
        g_wu = [mkc("g_wu0"), mkc("g_wu1")]
        l_cv = [mkc("l_cv0"), mkc("l_cv1")]
        l_st, l_cum, l_ccs, l_x1 = mkc("l_st"), mkc("l_cum"), mkc("l_ccs"), mkc("l_x1")
        l_uc, l_stc, l_dn = mkc("l_uc"), mkc("l_stc"), mkc("l_dn")
        l_gh = [mkc("l_gh0"), mkc("l_gh1")]
        y_k, y_ag, y_x1t, y_sdv = mkc("y_k"), mkc("y_ag"), mkc("y_x1t"), mkc("y_sdv")
        y_l = [mkc("y_l0"), mkc("y_l1")]
        y_tm = [mkc("y_tm0"), mkc("y_tm1")]
        y_e = [mkc("y_e0"), mkc("y_e1")]
        y_gc = [mkc("y_gc0"), mkc("y_gc1")]
        y_xd, y_out = mkc("y_xd"), mkc("y_out")
        l_xt = mkc("l_xt")
        y_h = mkc("y_h")
        ALLC = [c_pe, c_ve, c_ac, c_cc, c_gp, g_uT, g_wo, g_wd, g_wi[0], g_wi[1],
                g_wg[0], g_wg[1], g_wu[0], g_wu[1], l_cv[0], l_cv[1], l_st,
                l_cum, l_ccs, l_x1, l_uc, l_stc, l_dn, l_gh[0], l_gh[1],
                y_k, y_ag, y_l[0], y_l[1], y_x1t, y_sdv, y_tm[0], y_tm[1], y_e[0], y_e[1],
                y_gc[0], y_gc[1], y_xd, y_out, l_xt, y_h]

        block = es.enter_context(nc.Block())

        def snap():
            return [(c, c.n) for c in ALLC]

        def wait_all(eng, snapshot):
            for c, n in snapshot:
                if n > 0:
                    eng.wait_ge(c.sem, n)

        def wait_mark(eng, mark):
            ctr, val = mark
            if val > 0:
                eng.wait_ge(ctr.sem, val)

        @block.sync
        def _(sy):
            for dst, src in [(cw_sb, d_cw), (cb_sb, d_cb), (dpk_sb, d_dpk),
                             (zbp_sb, d_zbp), (sg_sb, d_sg), (su_sb, d_su),
                             (msk_sb, d_msk), (id_sb, d_id), (idb_sb, d_idb),
                             (ln2_sb, d_ln),
                             (ng_sb, d_ng), (on_sb, d_on)]:
                ap_d = dst[:, :, :] if len(dst.shape) == 3 else dst[:, :]
                y_k.inc(sy.dma_start(out=ap_d, in_=src), 16)
        syn_const = y_k.n

        @block.vector
        def _(v):
            v = _CH(v, c_ve)
            v.memset(epsc[:, 0:1], EPS)

        st15 = contextlib.ExitStack()
        dt_sb = st15.enter_context(nc.sbuf_tensor("dt_sb", [NH, T], F32))
        cumT = st15.enter_context(nc.sbuf_tensor("cumT", [NH, T], F32))
        st1 = contextlib.ExitStack()
        uT_sb = st1.enter_context(nc.sbuf_tensor("uT_sb", [P, NDT, T + 3], BF16))
        wi0 = st1.enter_context(nc.sbuf_tensor("wi0", [P, NDT, P], BF16))
        wi1 = st1.enter_context(nc.sbuf_tensor("wi1", [P, NDT, P], BF16))
        cv0 = st1.enter_context(nc.sbuf_tensor("cv0", [P, T], BF16))
        cv1 = st1.enter_context(nc.sbuf_tensor("cv1", [P, T], BF16))
        tapA = st1.enter_context(nc.sbuf_tensor("tapA", [P, T], F32))
        tapB = st1.enter_context(nc.sbuf_tensor("tapB", [P, T], F32))
        if True:
            wi = [wi0, wi1]
            cv = [cv0, cv1]
            tiles = [] if SKIP_A else [("A", 64, None, None)]
            if Z_ONLY:
                tiles += [("z", i, d_zsp, None) for i in range(48, 64)]
            tiles += [("c", i, d_xsp, i) for i in range(16)]
            tiles += [("c", i, d_bsp, i) for i in range(16, 32)]
            tiles += [("c", i, d_csp, i) for i in range(32, 48)]
            tiles += [("z", i, d_zsp, None) for i in range(48, 64)]
            tiles = tiles[:TILE_LIMIT]
            NT = len(tiles)

            CCH = [(0, 506, 0), (506, 1018, 1), (1018, 1530, 2), (1530, 2048, 4)]
            ZCH = [(0, 512, 0), (512, 1024, 1), (1024, 1536, 2), (1536, 2048, 3)]
            NEED_AFTER_CONV = {0: 2, 1: 3, 2: 4, 3: 4, 4: 4}
            NEED_AFTER_Z = {0: 1, 1: 2, 2: 3, 3: 4, 4: 4}
            if NO_OVERLAP:
                CCH = [(a, b, 4) for (a, b, g) in CCH]
                ZCH = [(a, b, 3) for (a, b, g) in ZCH]
                NEED_AFTER_CONV = {k: 4 for k in range(5)}
                NEED_AFTER_Z = {k: 4 for k in range(5)}

            uT_src = d_uT.rearrange("(dt p) t -> p dt t", p=P)
            wiT_src = d_wiT.rearrange("(dt p) c -> p dt c", p=P)

            @block.gpsimd
            def _(g):
                g = _CH(g, c_gp)
                for q in range(4):
                    g_uT.inc(g.dma_start(out=uT_sb[:, 4 * q:4 * (q + 1), :],
                                         in_=uT_src[:, 4 * q:4 * (q + 1), :]), 16)
            gld_uT = g_uT.n

            wi_marks = []
            grp_marks = []
            ev_marks = []
            spl_marks = []
            for ti, (kind, idx, spd, cr) in enumerate(tiles):
                ncols = 32 if (kind == "A" and not A_FULL) else P

                @block.gpsimd
                def _(g, ti=ti, ncols=ncols, idx=idx, kind=kind):
                    g = _CH(g, c_gp)
                    if ti >= 2:
                        g.wait_ge(c_pe.sem, grp_marks[ti - 2][-1])
                    col0 = idx * P if not (kind == "A" and A_FULL) else OUT_CH - P
                    g_wi[ti % 2].inc(g.dma_start(out=wi[ti % 2][:, :, 0:ncols],
                                                 in_=wiT_src[:, :, col0:col0 + ncols]), 16)
                    wi_marks.append(g_wi[ti % 2].mark())

                @block.tensor
                def _(t, ti=ti, kind=kind, ncols=ncols):
                    if ti == 0:
                        t.wait_ge(y_k.sem, syn_const)
                        t.wait_ge(g_uT.sem, gld_uT)
                    wait_mark(t, wi_marks[ti])
                    if kind in ("c", "A"):
                        groups = [(0, 512, 0), (512, 1024, 512), (1024, 1536, 1024),
                                  (1536, 2048, 1536), (2048, 2051, 2048)]
                    else:
                        groups = [(3, 515, 0), (515, 1027, 512), (1027, 1539, 1024),
                                  (1539, 2051, 1536)]
                    gm = []
                    prev_kind = tiles[ti - 1][0] if ti >= 1 else None
                    for gi, (a, b, pcol) in enumerate(groups):
                        if ti >= 1 and not NO_NEED:
                            need = (NEED_AFTER_CONV if prev_kind in ("c", "A")
                                    else NEED_AFTER_Z)[gi]
                            wait_mark(t, ev_marks[ti - 1][need - 1])
                        w = b - a
                        for dti in range(NDT):
                            inst = t.matmul(ps[0:ncols, pcol:pcol + w],
                                            wi[ti % 2][:, dti, 0:ncols],
                                            uT_sb[:, dti, a:b],
                                            start=(dti == 0), stop=(dti == NDT - 1))
                        c_pe.inc(inst, 1)
                        gm.append(c_pe.n)
                    if kind == "z":
                        gm.append(gm[-1])
                    grp_marks.append(gm)

                this_ev = []
                if SIMPLE_EVAC and kind == "c":
                    @block.vector
                    def _(v, ti=ti, cr=cr):
                        v = _CH(v, c_ve)
                        v.wait_ge(c_pe.sem, grp_marks[ti][4])
                        if ti >= 2:
                            wait_mark(v, spl_marks[ti - 2])
                        v.scalar_tensor_tensor(tapA[:, 0:T], ps[:, 3:3 + T],
                                               cw_sb[:, cr, 3:4], ps[:, 3:3 + T],
                                               ALU.mult, ALU.bypass)
                        v.scalar_tensor_tensor(tapB[:, 0:T], ps[:, 2:2 + T],
                                               cw_sb[:, cr, 2:3], tapA[:, 0:T],
                                               ALU.mult, ALU.add)
                        v.scalar_tensor_tensor(tapA[:, 0:T], ps[:, 1:1 + T],
                                               cw_sb[:, cr, 1:2], tapB[:, 0:T],
                                               ALU.mult, ALU.add)
                        v.tensor_scalar(tapB[:, 0:T], tapA[:, 0:T], cb_sb[:, cr:cr + 1],
                                        None, ALU.add)
                        v.scalar_tensor_tensor(cv[ti % 2][:, 0:T], ps[:, 0:T],
                                               cw_sb[:, cr, 0:1], tapB[:, 0:T],
                                               ALU.mult, ALU.add)
                    mkv = (c_ve, c_ve.n)
                    this_ev = [mkv, mkv, mkv, mkv]
                    ev_marks.append(this_ev)
                elif kind == "A":
                    @block.scalar
                    def _(s, ti=ti):
                        s = _CH(s, c_ac)
                        s.wait_ge(c_pe.sem, grp_marks[ti][4])
                        r0 = 96 if A_FULL else 0
                        if A_SIMPLE:
                            s.activation(tapA[0:32, 0:T], ps[r0:r0 + 32, 3:3 + T], AF.Relu)
                            inst = s.activation(dt_sb[0:32, :], ps[r0:r0 + 32, 3:3 + T], AF.Relu)
                        else:
                            s.activation(tapA[0:32, 0:T], ps[r0:r0 + 32, 3:3 + T], AF.Abs)
                            s.activation(tapA[0:32, 0:T], tapA[0:32, 0:T], AF.Exp, scale=-1.0)
                            s.activation(tapA[0:32, 0:T], tapA[0:32, 0:T], AF.Ln, bias=1.0)
                            inst = s.activation(dt_sb[0:32, :], ps[r0:r0 + 32, 3:3 + T], AF.Relu)
                    ac_A = c_ac.n

                    @block.vector
                    def _(v, ac_A=ac_A):
                        v = _CH(v, c_ve)
                        v.wait_ge(c_ac.sem, ac_A)
                        if A_NOIP:
                            inst = v.tensor_tensor(tapB[0:32, 0:T], dt_sb[0:32, :],
                                                   tapA[0:32, 0:T], ALU.add)
                        else:
                            inst = v.tensor_tensor(dt_sb[0:32, :], dt_sb[0:32, :],
                                                   tapA[0:32, 0:T], ALU.add)
                    for _k in range(4):
                        this_ev.append((c_ve, c_ve.n))
                    ev_marks.append(this_ev)
                    spl_marks.append((l_cv[0], 0))
                elif kind == "c":
                    for j, (a, b, gn) in enumerate(CCH):
                        if NO_ACT_TAP:
                            @block.vector
                            def _(v, ti=ti, j=j, a=a, b=b, gn=gn, cr=cr):
                                v = _CH(v, c_ve)
                                v.wait_ge(c_pe.sem, grp_marks[ti][gn])
                                if ti >= 1:
                                    wait_mark(v, ev_marks[ti - 1][j])
                                v.tensor_scalar(tapA[:, a:b], ps[:, a + 3:b + 3],
                                                cw_sb[:, cr, 3:4], cb_sb[:, cr:cr + 1],
                                                ALU.mult, ALU.add)
                        else:
                            @block.scalar
                            def _(s, ti=ti, j=j, a=a, b=b, gn=gn, cr=cr):
                                s = _CH(s, c_ac)
                                s.wait_ge(c_pe.sem, grp_marks[ti][gn])
                                if ti >= 1:
                                    wait_mark(s, ev_marks[ti - 1][j])
                                inst = s.activation(tapA[:, a:b], ps[:, a + 3:b + 3], AF.Identity,
                                                    bias=cb_sb[:, cr:cr + 1],
                                                    scale=cw_sb[:, cr, 3:4])
                        ac_j = c_ac.n

                        @block.vector
                        def _(v, ti=ti, j=j, a=a, b=b, cr=cr, ac_j=ac_j):
                            v = _CH(v, c_ve)
                            if not NO_ACT_TAP:
                                v.wait_ge(c_ac.sem, ac_j)
                            if j == 0 and ti >= 2:
                                wait_mark(v, spl_marks[ti - 2])
                            v.scalar_tensor_tensor(tapB[:, a:b], ps[:, a + 2:b + 2],
                                                   cw_sb[:, cr, 2:3], tapA[:, a:b],
                                                   ALU.mult, ALU.add)
                            v.scalar_tensor_tensor(tapA[:, a:b], ps[:, a + 1:b + 1],
                                                   cw_sb[:, cr, 1:2], tapB[:, a:b],
                                                   ALU.mult, ALU.add)
                            inst = v.scalar_tensor_tensor(cv[ti % 2][:, a:b], ps[:, a:b],
                                                          cw_sb[:, cr, 0:1], tapA[:, a:b],
                                                          ALU.mult, ALU.add)
                        this_ev.append((c_ve, c_ve.n))
                    ev_marks.append(this_ev)
                else:
                    zcol = idx - 48
                    for j, (a, b, gn) in enumerate(ZCH):
                        @block.scalar
                        def _(s, ti=ti, j=j, a=a, b=b, gn=gn, zcol=zcol):
                            s = _CH(s, c_ac)
                            s.wait_ge(c_pe.sem, grp_marks[ti][gn])
                            if j == 0 and ti >= 2:
                                wait_mark(s, spl_marks[ti - 2])
                            inst = s.activation(cv[ti % 2][:, a:b], ps[:, a:b], AF.Silu,
                                                bias=zbp_sb[:, zcol:zcol + 1])
                        this_ev.append((c_ac, c_ac.n))
                    ev_marks.append(this_ev)

                if kind != "A":
                    if ti % 2 == 0:
                        @block.scalar
                        def _(s, ti=ti, idx=idx, spd=spd):
                            s = _CH(s, c_ac)
                            wait_mark(s, ev_marks[ti][3])
                            ct0 = idx % 16
                            l_cv[0].inc(s.dma_start(
                                out=spd.rearrange("(ci p) (ct t) -> p ct ci t",
                                                  p=P, t=CL)[:, ct0, :, :],
                                in_=cv[0][:, :].rearrange("p (ci t) -> p ci t", ci=NCH)), 16)
                    else:
                        @block.sync
                        def _(sy, ti=ti, idx=idx, spd=spd):
                            wait_mark(sy, ev_marks[ti][3])
                            ct0 = idx % 16
                            l_cv[1].inc(sy.dma_start(
                                out=spd.rearrange("(ci p) (ct t) -> p ct ci t",
                                                  p=P, t=CL)[:, ct0, :, :],
                                in_=cv[1][:, :].rearrange("p (ci t) -> p ci t", ci=NCH)), 16)
                    spl_marks.append(l_cv[ti % 2].mark())

            s1_end = snap()

        if STAGE_LIMIT < 2:
            _finalize(block, ALLC)
            st1.close()
            st15.close()
            return nc

        st1.close()
        st15b = contextlib.ExitStack()
        dttm = st15b.enter_context(nc.sbuf_tensor("dttm", [P, NH], F32))
        sdt0 = st15b.enter_context(nc.sbuf_tensor("sdt0", [NH, CL], F32))
        sdtm = st15b.enter_context(nc.sbuf_tensor("sdtm", [P, NH], F32))
        xtm0 = st15b.enter_context(nc.sbuf_tensor("xtm0", [P, T], BF16))
        xtm1 = st15b.enter_context(nc.sbuf_tensor("xtm1", [P, T], BF16))
        btm0 = st15b.enter_context(nc.sbuf_tensor("btm0", [P, T], BF16))
        btm1 = st15b.enter_context(nc.sbuf_tensor("btm1", [P, T], BF16))
        xtm2 = st15b.enter_context(nc.sbuf_tensor("xtm2", [P, T], BF16))
        btm2 = st15b.enter_context(nc.sbuf_tensor("btm2", [P, T], BF16))
        xcm0 = st15b.enter_context(nc.sbuf_tensor("xcm0", [P, T], BF16))
        xcm1 = st15b.enter_context(nc.sbuf_tensor("xcm1", [P, T], BF16))
        xcm2 = st15b.enter_context(nc.sbuf_tensor("xcm2", [P, T], BF16))
        bcm0 = st15b.enter_context(nc.sbuf_tensor("bcm0", [P, T], BF16))
        bcm1 = st15b.enter_context(nc.sbuf_tensor("bcm1", [P, T], BF16))
        bcm2 = st15b.enter_context(nc.sbuf_tensor("bcm2", [P, T], BF16))
        bsd = st15b.enter_context(nc.sbuf_tensor("bsd", [P, T], BF16))
        stsb = st15b.enter_context(nc.sbuf_tensor("stsb", [P, 1024], F32))
        agp = st15b.enter_context(nc.sbuf_tensor("agp", [P, 1024], F32))
        dtm_all = st15b.enter_context(nc.sbuf_tensor("dtm_all", [P, NCH * NH], F32))
        clb = st15b.enter_context(nc.sbuf_tensor("clb", [P, NH], F32))
        sdb15 = st15b.enter_context(nc.sbuf_tensor("sdb15", [P, T], F32))
        if True:
            xtm = [xtm0, xtm1, xtm2]
            btm = [btm0, btm1, btm2]
            xcm15 = [xcm0, xcm1, xcm2]
            bcm15 = [bcm0, bcm1, bcm2]

            @block.vector
            def _(v):
                v = _CH(v, c_ve)
                wait_all(v, s1_end)
                v.memset(prev[:, :], 0.0)
                inst = v.tensor_copy(prevb[:, :], prev[:, :])

            @block.tensor
            def _(t):
                wait_all(t, s1_end)
                for ci in range(NCH):
                    inst = t.transpose(ps[:, 3584 + ci * NH:3584 + (ci + 1) * NH],
                                       dt_sb[:, ci * CL:(ci + 1) * CL], id_sb[0:32, 0:32])
                c_pe.inc(inst, 1)
            tr_pe = c_pe.n

            @block.vector
            def _(v):
                v = _CH(v, c_ve)
                v.wait_ge(c_pe.sem, tr_pe)
                inst = v.tensor_copy(dtm_all[:, :], ps[:, 3584:3584 + NCH * NH])
            dttm_ve = c_ve.n

            @block.tensor
            def _(t):
                t.wait_ge(c_ve.sem, dttm_ve)
                for ci in range(NCH):
                    t.matmul(ps[0:32, ci * CL:(ci + 1) * CL],
                             dtm_all[:, ci * NH:(ci + 1) * NH], ln2_sb[:, :],
                             start=True, stop=True)
                for ci in range(NCH):
                    inst = t.matmul(ps[:, 2048 + ci * NH:2048 + (ci + 1) * NH],
                                    ln2_sb[:, :], dtm_all[:, ci * NH:(ci + 1) * NH],
                                    start=True, stop=True)
                c_pe.inc(inst, 1)
            mm_pe = c_pe.n

            @block.vector
            def _(v):
                v = _CH(v, c_ve)
                v.wait_ge(c_pe.sem, mm_pe)
                v.tensor_copy(cumT[:, :], ps[0:32, 0:T])
                inst = v.tensor_copy(cumtm[:, :], ps[:, 2048:2048 + NCH * NH])
            cum_ve = c_ve.n

            @block.scalar
            def _(s):
                s = _CH(s, c_ac)
                s.wait_ge(c_ve.sem, cum_ve)
                l_cum.inc(s.dma_start(out=d_cum[:, :], in_=cumT[:, :]), 16)
                l_cum.inc(s.dma_start(
                    out=d_cumB.rearrange("ci h t -> h ci t"),
                    in_=cumT[:, :].rearrange("h (ci t) -> h ci t", ci=NCH)), 16)
                cta = cumT.ap()
                l_cum.inc(s.dma_start(
                    out=d_cumL.rearrange("ci h -> h ci"),
                    in_=AP(cta.tensor, cta.offset + CL - 1, [cta.ap[0], [CL, NCH]]),
                    allow_slow_non_contiguous=True), 16)
            scl_cum = l_cum.n
            cv_tot0, cv_tot1 = l_cv[0].n, l_cv[1].n

            cum_src = d_cum.rearrange("h t -> (h t)")
            cuml_src = d_cumL.rearrange("ci h -> (ci h)")
            carry_ve = 0
            st_pe = 0
            sd_ve = 0
            sd_ac = 0
            st_pe_hist = {}
            sd_ac_hist = {}
            tpcopy_hist = {}
            tmuse_hist = {}
            for ci in range(NCH):
                cs = ci * CL
                pf = ci % 3

                @block.sync
                def _(sy, ci=ci, cs=cs, pf=pf, st_pe=st_pe):
                    if ci == 0:
                        sy.wait_ge(l_cum.sem, scl_cum)
                        sy.wait_ge(l_cv[0].sem, cv_tot0)
                        sy.wait_ge(l_cv[1].sem, cv_tot1)
                    if ci >= 3:
                        sy.wait_ge(c_pe.sem, st_pe_hist[ci - 2])
                    if ci >= 1:
                        sy.wait_ge(c_ve.sem, sd_ve)
                    if ci >= 1:
                        sy.wait_ge(c_ac.sem, sd_ac_hist[ci - 1])
                    y_tm[ci % 2].inc(sy.dma_start(
                        out=xcm15[pf][:, :],
                        in_=d_xsp.rearrange("(ci p) c -> ci p c", p=P)[ci]), 16)
                    y_tm[ci % 2].inc(sy.dma_start(
                        out=bcm15[pf][:, :],
                        in_=d_bsp.rearrange("(ci p) c -> ci p c", p=P)[ci]), 16)
                    apin = AP(cuml_src.tensor, cuml_src.offset + ci * NH, [[0, P], [1, NH]])
                    y_tm[ci % 2].inc(sy.dma_start(out=clb[:, :], in_=apin), 16)
                strm_syn = y_tm[ci % 2].mark()

                @block.tensor
                def _(t, ci=ci, pf=pf, strm_syn=strm_syn):
                    wait_mark(t, strm_syn)
                    if ci >= 2:
                        t.wait_ge(c_ve.sem, tpcopy_hist[ci - 1])
                    pb = ps[:, 0:1024].bitcast(BF16)
                    pb2 = ps[:, 1024:2048].bitcast(BF16)
                    for ct in range(16):
                        t.transpose(pb[:, ct * CL:(ct + 1) * CL],
                                    xcm15[pf][:, ct * CL:(ct + 1) * CL], idb_sb[:, :])
                    for ct in range(16):
                        inst = t.transpose(pb2[:, ct * CL:(ct + 1) * CL],
                                           bcm15[pf][:, ct * CL:(ct + 1) * CL], idb_sb[:, :])
                    c_pe.inc(inst, 1)
                tp_pe = c_pe.n

                @block.vector
                def _(v, ci=ci, pf=pf, tp_pe=tp_pe):
                    v = _CH(v, c_ve)
                    v.wait_ge(c_pe.sem, tp_pe)
                    if ci >= 3:
                        wait_mark(v, tmuse_hist[ci - 2])
                    v.tensor_copy(xtm[pf][:, :], ps[:, 0:1024].bitcast(BF16))
                    inst = v.tensor_copy(btm[pf][:, :], ps[:, 1024:2048].bitcast(BF16))
                tpcopy = c_ve.n
                tpcopy_hist[ci] = tpcopy

                @block.vector
                def _(v, ci=ci, cs=cs, carry_ve=carry_ve, strm_syn=strm_syn, pf=pf):
                    v = _CH(v, c_ve)
                    if ci >= 1:
                        v.wait_ge(c_ve.sem, carry_ve)
                    wait_mark(v, strm_syn)
                    ca = clb.ap()
                    c3 = AP(ca.tensor, ca.offset, [ca.ap[0], [1, NH], [0, HD]])
                    ma = cumtm.ap()
                    m3 = AP(ma.tensor, ma.offset + ci * NH, [ma.ap[0], [1, NH], [0, HD]])
                    inst = v.tensor_tensor(
                        sdb15.ap().rearrange("p (h d) -> p h d", h=NH), c3, m3,
                        ALU.subtract)
                sd_ve = c_ve.n

                @block.scalar
                def _(s, ci=ci, pf=pf, sd_ve=sd_ve):
                    s = _CH(s, c_ac)
                    s.wait_ge(c_ve.sem, sd_ve)
                    s.activation(sdb15[:, :], sdb15[:, :], AF.Exp)
                    for blk in range(2):
                        pa = clb[blk * 64:(blk + 1) * 64, :]
                        inst = s.activation(cdv[ci % 2][blk * 64:(blk + 1) * 64, :],
                                            AP(pa.tensor, pa.offset + blk,
                                               [pa.ap[0], [2, 16]]), AF.Exp)
                sd_ac = c_ac.n
                sd_ac_hist[ci] = sd_ac

                @block.vector
                def _(v, pf=pf, sd_ac=sd_ac, tpcopy=tpcopy):
                    v = _CH(v, c_ve)
                    v.wait_ge(c_ac.sem, sd_ac)
                    v.wait_ge(c_ve.sem, tpcopy)
                    inst = v.tensor_tensor(bsd[:, :], btm[pf][:, :], sdb15[:, :], ALU.mult)
                bsd_ve = c_ve.n

                @block.tensor
                def _(t, pf=pf, bsd_ve=bsd_ve, ci=ci, carry_ve=carry_ve):
                    t.wait_ge(c_ve.sem, bsd_ve)
                    if ci >= 1:
                        t.wait_ge(c_ve.sem, carry_ve)
                    for h in range(NH):
                        rb = HD * (h % 2)
                        inst = t.matmul(ps[rb:rb + HD,
                                           2560 + (h // 2) * HD:2560 + (h // 2) * HD + HD],
                                        bsd[:, h * HD:(h + 1) * HD],
                                        xtm[pf][:, h * HD:(h + 1) * HD],
                                        start=True, stop=True, tile_position=(0, rb))
                    c_pe.inc(inst, 1)
                st_pe = c_pe.n
                st_pe_hist[ci] = st_pe
                tmuse_hist[ci] = (c_pe, st_pe)

                @block.vector
                def _(v, pf=pf, st_pe=st_pe, _lst=None):
                    v = _CH(v, c_ve)
                    v.wait_ge(c_pe.sem, st_pe)
                    if l_st.n > 0:
                        v.wait_ge(l_st.sem, l_st.n)
                    v.tensor_copy(stsb[:, :], ps[:, 2560:2560 + 1024])
                    cda = cdv[ci % 2].ap()
                    cd3 = AP(cda.tensor, cda.offset, [cda.ap[0], [1, 16], [0, HD]])
                    v.tensor_tensor(prev.ap().rearrange("p (r d) -> p r d", r=16),
                                    prev.ap().rearrange("p (r d) -> p r d", r=16),
                                    cd3, ALU.mult)
                    inst = v.tensor_tensor(prev[:, :], prev[:, :], stsb[:, :], ALU.add)
                carry_ve = c_ve.n

                @block.scalar
                def _(s, ci=ci, carry_ve=carry_ve):
                    s = _CH(s, c_ac)
                    s.wait_ge(c_ve.sem, carry_ve)
                    l_st.inc(s.dma_start(out=d_stsp[:, ci * 1024:(ci + 1) * 1024],
                                         in_=stsb[:, :]), 16)

            @block.scalar
            def _(s, carry_ve=carry_ve):
                s = _CH(s, c_ac)
                s.wait_ge(c_ve.sem, carry_ve)
                l_ccs.inc(s.dma_start(out=d_ccs[:, :], in_=prev[:, :]), 16)
            scl_ccs = l_ccs.n

            @block.gpsimd
            def _(g):
                g = _CH(g, c_gp)
                g.wait_ge(l_ccs.sem, scl_ccs)
                inst = g.collective_compute(
                    "AllGather", ALU.bypass,
                    replica_groups=[[0, 1], [2, 3], [4, 5], [6, 7]],
                    ins=[d_ccs.opt()], outs=[d_ccg.opt()])
                c_cc.inc(inst, 1)

            @block.sync
            def _(sy):
                sy.wait_ge(c_cc.sem, c_cc.n)
                y_ag.inc(sy.dma_start(out=agp[:, :], in_=d_ccg[0:P, :]), 16)
            syn_agp = y_ag.n

            @block.vector
            def _(v):
                v = _CH(v, c_ve)
                v.wait_ge(y_ag.sem, syn_agp)
                v.tensor_scalar(prev[:, :], agp[:, :], msk_sb[:, :], None, ALU.mult)
                inst = v.tensor_copy(prevb[:, :], prev[:, :])
            s15_end = snap()
        st15b.close()
        st15.close()
        if STAGE_LIMIT < 3:
            _finalize(block, ALLC)
            return nc

        st2 = contextlib.ExitStack()
        woT_sb = st2.enter_context(nc.sbuf_tensor("woT_sb", [P, NDT, D], BF16))
        xtc0 = st2.enter_context(nc.sbuf_tensor("xtc0", [P, T], BF16))
        xtc1 = st2.enter_context(nc.sbuf_tensor("xtc1", [P, T], BF16))
        btc0 = st2.enter_context(nc.sbuf_tensor("btc0", [P, T], BF16))
        btc1 = st2.enter_context(nc.sbuf_tensor("btc1", [P, T], BF16))
        ctc0 = st2.enter_context(nc.sbuf_tensor("ctc0", [P, T], BF16))
        ctc1 = st2.enter_context(nc.sbuf_tensor("ctc1", [P, T], BF16))
        szc0 = st2.enter_context(nc.sbuf_tensor("szc0", [P, T], BF16))
        szc1 = st2.enter_context(nc.sbuf_tensor("szc1", [P, T], BF16))
        xmc = st2.enter_context(nc.sbuf_tensor("xmc", [P, T], BF16))
        hbc = st2.enter_context(nc.sbuf_tensor("hbc", [P, 4096], F32))
        uc = st2.enter_context(nc.sbuf_tensor("uc", [P, D], F32))
        stc = st2.enter_context(nc.sbuf_tensor("stc", [P, 1024], F32))
        ebp = st2.enter_context(nc.sbuf_tensor("ebp", [P, T], BF16))
        cxp = st2.enter_context(nc.sbuf_tensor("cxp", [P, T], BF16))
        dif = st2.enter_context(nc.sbuf_tensor("dif", [P, T], F32))
        dcy = st2.enter_context(nc.sbuf_tensor("dcy", [P, T], BF16))
        g2m = st2.enter_context(nc.sbuf_tensor("g2m", [P, T], BF16))
        ytp = st2.enter_context(nc.sbuf_tensor("ytp", [P, D], BF16))
        ygt = st2.enter_context(nc.sbuf_tensor("ygt", [P, D], BF16))
        r1s = st2.enter_context(nc.sbuf_tensor("r1s", [P, D], F32))
        x1s = st2.enter_context(nc.sbuf_tensor("x1s", [P, D], F32))
        x1bs0 = st2.enter_context(nc.sbuf_tensor("x1bs0", [P, D], BF16))
        x1bs1 = st2.enter_context(nc.sbuf_tensor("x1bs1", [P, D], BF16))
        x1ts = st2.enter_context(nc.sbuf_tensor("x1ts", [P, D], BF16))
        if True:
            btc = [btc0, btc1]
            ctc = [ctc0, ctc1]
            xtc = [xtc0, xtc1]
            szc = [szc0, szc1]
            x1bs = [x1bs0, x1bs1]

            @block.gpsimd
            def _(g):
                g = _CH(g, c_gp)
                wait_all(g, s15_end)
                for q in range(4):
                    g_wo.inc(g.dma_start(
                        out=woT_sb[:, 4 * q:4 * (q + 1), :],
                        in_=d_woT.rearrange("(it p) o -> p it o", p=P)[:, 4 * q:4 * (q + 1), :]), 16)
            gld_wo = g_wo.n

            cum_src = d_cum.rearrange("h t -> (h t)")
            cumb_src = d_cumB.rearrange("ci h t -> (ci h t)")
            syn_early = {}
            syn_late = {}
            syn_hb = {}
            scl_uc = {}
            scl_st = {}

            def issue_early(sy, ci):
                pf = ci % 2
                y_e[pf].inc(sy.dma_start(out=btc[pf][:, :],
                                         in_=d_bsp.rearrange("(ci p) c -> ci p c", p=P)[ci]), 16)
                y_e[pf].inc(sy.dma_start(out=ctc[pf][:, :],
                                         in_=d_csp.rearrange("(ci p) c -> ci p c", p=P)[ci]), 16)
                syn_early[ci] = y_e[pf].mark()

            def issue_late(sy, ci):
                pf = ci % 2
                y_l[pf].inc(sy.dma_start(out=xtc[pf][:, :],
                                     in_=d_xsp.rearrange("(ci p) c -> ci p c", p=P)[ci]), 16)
                y_l[pf].inc(sy.dma_start(out=szc[pf][:, :],
                                     in_=d_zsp.rearrange("(ci p) c -> ci p c", p=P)[ci]), 16)
                syn_late[ci] = y_l[pf].mark()

            def issue_hb(sy, ci):
                apin = AP(cumb_src.tensor, cumb_src.offset + ci * NH * CL,
                          [[0, P], [1, NH * CL]])
                y_h.inc(sy.dma_start(out=hbc[:, :], in_=apin), 16)
                syn_hb[ci] = y_h.mark()

            @block.sync
            def _(sy):
                wait_all(sy, s15_end)
                issue_early(sy, 0)
                issue_late(sy, 0)
                issue_hb(sy, 0)
                issue_early(sy, 1)
                issue_late(sy, 1)

            @block.scalar
            def _(s):
                s = _CH(s, c_ac)
                wait_all(s, s15_end)
                cs = 0
                l_uc.inc(s.dma_start(out=uc[:, :], in_=d_ur[cs:cs + CL, :]), 16)
                scl_uc[0] = l_uc.n
                l_stc.inc(s.dma_start(out=stc[:, :], in_=d_stsp[:, 0:1024]), 16)
                scl_st[0] = l_stc.n

            prevb_ve = c_ve.n
            x1tr_pend = None
            ln_pend = None
            x1_ve = 0
            x1t_ve_m = [0]
            x1bs_free = {0: 0, 1: 0}
            ln_ac_hist = {}
            ebp_ac = 0
            cxp_ve = 0
            dif_ve = 0
            dcy_ac = 0
            g2_pe = 0
            g2m_ve = 0
            y_pe = 0
            gat_ve = 0
            wo_pe = 0
            wo_r1_ve = 0
            carry2_ve = 0
            carry_hist = {}
            chunk_ve = {}
            chunk_ac = {}

            def emit_ln(pci, p_wo_r1, x1_ve_prev):
                @block.scalar
                def _(s, p_wo_r1=p_wo_r1, x1_ve_prev=x1_ve_prev):
                    s = _CH(s, c_ac)
                    s.wait_ge(c_ve.sem, p_wo_r1)
                    if x1_ve_prev > 0:
                        s.wait_ge(c_ve.sem, x1_ve_prev)
                    s.activation(ytp[:, :], r1s[:, :], AF.Square, scale=RS,
                                 accum_out=stat[:, 0:1])
                    inst = s.activation(ytp[:, :], r1s[:, :], AF.Identity, scale=RT,
                                        accum_out=stat[:, 1:2])
                ln_ac = c_ac.n

                @block.vector
                def _(v, ln_ac=ln_ac):
                    v = _CH(v, c_ve)
                    v.wait_ge(c_ac.sem, ln_ac)
                    inst = v.scalar_tensor_tensor(stat[:, 2:3], stat[:, 1:2], stat[:, 1:2],
                                                  stat[:, 0:1], ALU.mult, ALU.subtract)
                nv_ve = c_ve.n

                @block.scalar
                def _(s, nv_ve=nv_ve):
                    s = _CH(s, c_ac)
                    s.wait_ge(c_ve.sem, nv_ve)
                    s.activation(stat[:, 3:4], stat[:, 2:3], AF.Ln, scale=-1.0, bias=EPS)
                    inst = s.activation(stat[:, 4:5], stat[:, 3:4], AF.Exp, scale=-0.5)
                rstd_ac = c_ac.n

                @block.vector
                def _(v, pci=pci, rstd_ac=rstd_ac):
                    v = _CH(v, c_ve)
                    v.wait_ge(c_ac.sem, rstd_ac)
                    if l_x1.n > 0:
                        v.wait_ge(l_x1.sem, l_x1.n)
                    if x1bs_free[pci % 2] > 0:
                        v.wait_ge(c_pe.sem, x1bs_free[pci % 2])
                    v.tensor_scalar(x1s[:, :], r1s[:, :], stat[:, 1:2], stat[:, 4:5],
                                    ALU.subtract, ALU.mult)
                    inst = v.tensor_copy(x1bs[pci % 2][:, :], x1s[:, :])
                x1_ve_new = c_ve.n

                @block.scalar
                def _(s, pci=pci, x1_ve_new=x1_ve_new):
                    s = _CH(s, c_ac)
                    s.wait_ge(c_ve.sem, x1_ve_new)
                    l_x1.inc(s.dma_start(out=d_x1[pci * CL:(pci + 1) * CL, :],
                                         in_=x1s[:, :]), 16)
                return x1_ve_new, ln_ac

            def consume_x1t(wo_r1_now):
                pci, px1ve = x1tr_pend[0], x1tr_pend[1]

                @block.tensor
                def _(t, pci=pci, px1ve=px1ve, wo_r1_now=wo_r1_now):
                    t.wait_ge(c_ve.sem, px1ve)
                    if wo_r1_now > 0:
                        t.wait_ge(c_ve.sem, wo_r1_now)
                    pb = ps[:, 1024:2048].bitcast(BF16)
                    for ct in range(16):
                        inst = t.transpose(pb[:, ct * CL:(ct + 1) * CL],
                                           x1bs[pci % 2][:, ct * CL:(ct + 1) * CL],
                                           idb_sb[:, :])
                    c_pe.inc(inst, 1)
                x1t_pe = c_pe.n
                x1bs_free[pci % 2] = x1t_pe

                @block.vector
                def _(v, x1t_pe=x1t_pe):
                    v = _CH(v, c_ve)
                    v.wait_ge(c_pe.sem, x1t_pe)
                    if l_xt.n > 0:
                        v.wait_ge(l_xt.sem, l_xt.n)
                    inst = v.tensor_copy(x1ts[:, :], ps[:, 1024:2048].bitcast(BF16))
                x1t_ve_m[0] = c_ve.n

                @block.scalar
                def _(s, pci=pci):
                    s = _CH(s, c_ac)
                    s.wait_ge(c_ve.sem, x1t_ve_m[0])
                    l_xt.inc(s.dma_start(
                        out=d_x1t.rearrange("(ci p) c -> ci p c", p=P)[pci],
                        in_=x1ts[:, :]), 16)

            for ci in range(min(NCH, S2_CHUNKS)):
                cs = ci * CL
                pf = ci % 2

                @block.scalar
                def _(s, ci=ci, pf=pf, cxp_ve=cxp_ve):
                    s = _CH(s, c_ac)
                    wait_mark(s, syn_hb[ci])
                    if ci >= 1:
                        s.wait_ge(c_ve.sem, cxp_ve)
                    if ci >= 2:
                        s.wait_ge(c_ve.sem, carry_hist[ci - 2])
                    for blk in range(2):
                        pa = hbc[blk * 64:(blk + 1) * 64, :]
                        s.activation(ebp[blk * 64:(blk + 1) * 64, :],
                                     AP(pa.tensor, pa.offset + blk * CL,
                                        [pa.ap[0], [2 * CL, 16], [1, CL]]), AF.Exp)
                    for blk in range(2):
                        pa = hbc[blk * 64:(blk + 1) * 64, :]
                        inst = s.activation(cdv[pf][blk * 64:(blk + 1) * 64, :],
                                            AP(pa.tensor, pa.offset + blk * CL + CL - 1,
                                               [pa.ap[0], [2 * CL, 16]]), AF.Exp)
                ebp_ac = c_ac.n

                @block.vector
                def _(v, ci=ci, pf=pf, ebp_ac=ebp_ac, y_pe=y_pe):
                    v = _CH(v, c_ve)
                    wait_mark(v, syn_early[ci])
                    v.wait_ge(c_ac.sem, ebp_ac)
                    if ci >= 1:
                        v.wait_ge(c_pe.sem, y_pe)
                    inst = v.tensor_tensor(cxp[:, :], ctc[pf][:, :], ebp[:, :], ALU.mult)
                cxp_ve = c_ve.n

                @block.tensor
                def _(t, ci=ci, wo_r1_ve=wo_r1_ve):
                    wait_mark(t, syn_late[ci])
                    if ci > 0:
                        t.wait_ge(c_ve.sem, wo_r1_ve)
                    pb = ps[:, 0:1024].bitcast(BF16)
                    for ct in range(16):
                        inst = t.transpose(pb[:, ct * CL:(ct + 1) * CL],
                                           xtc[ci % 2][:, ct * CL:(ct + 1) * CL], idb_sb[:, :])
                    c_pe.inc(inst, 1)
                xm_pe = c_pe.n

                @block.vector
                def _(v, xm_pe=xm_pe, ci=ci, y_pe_prev=0):
                    v = _CH(v, c_ve)
                    v.wait_ge(c_pe.sem, xm_pe)
                    if ci > 0:
                        v.wait_ge(c_pe.sem, y_pe)
                    inst = v.tensor_copy(xmc[:, :], ps[:, 0:1024].bitcast(BF16))
                xm_ve = c_ve.n

                for hf in range(2):
                    @block.vector
                    def _(v, ci=ci, hf=hf, dcy_ac=dcy_ac):
                        v = _CH(v, c_ve)
                        if ci > 0 or hf > 0:
                            v.wait_ge(c_ac.sem, dcy_ac)
                        if hf == 0:
                            wait_mark(v, syn_hb[ci])
                        da = dif.ap()
                        d4 = AP(da.tensor, da.offset, [da.ap[0], [512, 4], [CL, 4], [1, CL]])
                        ha = hbc.ap()
                        h4 = AP(ha.tensor, ha.offset + hf * 2048,
                                [ha.ap[0], [CL, 4], [512, 4], [1, CL]])
                        ca = cumtm.ap()
                        c4 = AP(ca.tensor, ca.offset + ci * NH + 16 * hf,
                                [ca.ap[0], [1, 4], [4, 4], [0, CL]])
                        na = ng_sb.ap()
                        n4 = AP(na.tensor, na.offset, [na.ap[0], [0, 4], [0, 4], [1, CL]])
                        v.tensor_tensor(d4, h4, c4, ALU.subtract)
                        inst = v.tensor_tensor(d4, d4, n4, ALU.add)
                    dif_ve = c_ve.n

                    @block.scalar
                    def _(s, dif_ve=dif_ve, g2m_ve=g2m_ve, ci=ci, hf=hf):
                        s = _CH(s, c_ac)
                        s.wait_ge(c_ve.sem, dif_ve)
                        if ci > 0 or hf > 0:
                            s.wait_ge(c_ve.sem, g2m_ve)
                        inst = s.activation(dcy[:, :], dif[:, :], AF.Exp)
                    dcy_ac = c_ac.n

                    @block.tensor
                    def _(t, ci=ci, pf=pf, hf=hf, g2m_ve=g2m_ve):
                        wait_mark(t, syn_early[ci])
                        if ci > 0 or hf > 0:
                            t.wait_ge(c_ve.sem, g2m_ve)
                        if not NO_SSD_MM:
                            for hh in range(16):
                                h = 16 * hf + hh
                                rb = HD * (h % 2)
                                gc = 2048 + _g2col(hh)
                                inst = t.matmul(ps[:, gc:gc + CL],
                                                btc[pf][rb:rb + HD, (h // 2) * CL:(h // 2) * CL + CL],
                                                ctc[pf][rb:rb + HD, (h // 2) * CL:(h // 2) * CL + CL],
                                                start=True, stop=True, tile_position=(rb, 0))
                            c_pe.inc(inst, 1)
                        else:
                            inst = t.matmul(ps[:, 2048:2048 + CL], btc[pf][0:P, 0:CL],
                                            ctc[pf][0:P, 0:CL], start=True, stop=True)
                            c_pe.inc(inst, 1)
                    g2_pe = c_pe.n

                    if hf == 0 and x1tr_pend is not None:
                        consume_x1t(wo_r1_ve)
                        x1tr_pend = None

                    @block.vector
                    def _(v, g2_pe=g2_pe, dcy_ac=dcy_ac, y_pe=y_pe, ci=ci, hf=hf):
                        v = _CH(v, c_ve)
                        v.wait_ge(c_pe.sem, g2_pe)
                        v.wait_ge(c_ac.sem, dcy_ac)
                        if ci > 0 or hf > 0:
                            v.wait_ge(c_pe.sem, y_pe)
                        inst = v.tensor_tensor(g2m[:, :], ps[:, 2048:4096], dcy[:, :], ALU.mult)
                    g2m_ve = c_ve.n

                    @block.tensor
                    def _(t, ci=ci, hf=hf, g2m_ve=g2m_ve, cxp_ve=cxp_ve,
                          prevb_ve=prevb_ve, xm_ve=xm_ve):
                        t.wait_ge(c_ve.sem, g2m_ve)
                        t.wait_ge(c_ve.sem, cxp_ve)
                        t.wait_ge(c_ve.sem, prevb_ve)
                        t.wait_ge(c_ve.sem, xm_ve)
                        if x1t_ve_m[0] > 0:
                            v_ = x1t_ve_m[0]
                            t.wait_ge(c_ve.sem, v_)
                        if not NO_SSD_MM:
                            for hh in range(16):
                                h = 16 * hf + hh
                                rb = HD * (h % 2)
                                pr = h // 2
                                t.matmul(ps[rb:rb + HD, pr * CL:pr * CL + CL],
                                         xmc[:, h * HD:(h + 1) * HD],
                                         g2m[:, _g2col(hh):_g2col(hh) + CL],
                                         start=True, stop=False, tile_position=(0, rb))
                                inst = t.matmul(ps[rb:rb + HD, pr * CL:pr * CL + CL],
                                                prevb[rb:rb + HD, pr * HD:pr * HD + HD],
                                                cxp[rb:rb + HD, pr * CL:pr * CL + CL],
                                                start=False, stop=True, tile_position=(rb, rb))
                            c_pe.inc(inst, 1)
                        else:
                            inst = t.matmul(ps[0:P, 0:CL], xmc[:, 0:P], g2m[:, 0:CL],
                                            start=True, stop=True)
                            c_pe.inc(inst, 1)
                    y_pe = c_pe.n

                if ci + 1 < min(NCH, S2_CHUNKS):
                    @block.sync
                    def _(sy, ci=ci, dif_ve=dif_ve, ebp_ac=ebp_ac):
                        sy.wait_ge(c_ve.sem, dif_ve)
                        sy.wait_ge(c_ac.sem, ebp_ac)
                        issue_hb(sy, ci + 1)

                if ln_pend is not None:
                    lpci, p_wo_r1 = ln_pend
                    x1_ve, _lnac = emit_ln(lpci, p_wo_r1, x1_ve)
                    ln_ac_hist[lpci] = _lnac
                    x1tr_pend = (lpci, x1_ve)
                    ln_pend = None

                @block.vector
                def _(v, ci=ci, y_pe=y_pe, wo_pe=wo_pe):
                    v = _CH(v, c_ve)
                    v.wait_ge(c_pe.sem, y_pe)
                    wait_mark(v, syn_late[ci])
                    if ci >= 1:
                        v.wait_ge(c_pe.sem, wo_pe)
                        v.wait_ge(c_ac.sem, ln_ac_hist[ci - 1])
                    xa = xtc[ci % 2].ap().rearrange("p (r q) -> p r q", r=16)
                    dpa = dpk_sb.ap()
                    dp3 = AP(dpa.tensor, dpa.offset, [dpa.ap[0], [1, 16], [0, CL]])
                    v.tensor_tensor(ytp.ap().rearrange("p (r q) -> p r q", r=16),
                                    xa, dp3, ALU.mult)
                    v.tensor_tensor(ytp[:, :], ytp[:, :], ps[:, 0:D], ALU.add)
                    inst = v.tensor_tensor(ygt[:, :], ytp[:, :], szc[ci % 2][:, :], ALU.mult)
                gat_ve = c_ve.n

                @block.vector
                def _(v, pf=pf, ebp_ac=ebp_ac, y_pe=y_pe, ci=ci):
                    v = _CH(v, c_ve)
                    v.wait_ge(c_ac.sem, ebp_ac)
                    v.wait_ge(c_pe.sem, y_pe)
                    v.wait_ge(l_stc.sem, scl_st[ci])
                    cda = cdv[pf].ap()
                    cd3 = AP(cda.tensor, cda.offset, [cda.ap[0], [1, 16], [0, HD]])
                    v.tensor_tensor(prev.ap().rearrange("p (r d) -> p r d", r=16),
                                    prev.ap().rearrange("p (r d) -> p r d", r=16),
                                    cd3, ALU.mult)
                    v.tensor_tensor(prev[:, :], prev[:, :], stc[:, :], ALU.add)
                    inst = v.tensor_copy(prevb[:, :], prev[:, :])
                carry2_ve = c_ve.n
                prevb_ve = carry2_ve
                carry_hist[ci] = carry2_ve

                if ci + 2 < min(NCH, S2_CHUNKS):
                    @block.sync
                    def _(sy, ci=ci, gat_ve=gat_ve, ebp_ac=ebp_ac):
                        sy.wait_ge(c_ve.sem, gat_ve)
                        sy.wait_ge(c_ac.sem, ebp_ac)
                        if ci >= 1:
                            sy.wait_ge(c_ve.sem, chunk_ve[ci - 1])
                            sy.wait_ge(c_ac.sem, chunk_ac[ci - 1])
                        issue_early(sy, ci + 2)
                        issue_late(sy, ci + 2)

                @block.tensor
                def _(t, ci=ci, gat_ve=gat_ve):
                    t.wait_ge(c_ve.sem, gat_ve)
                    if ci == 0:
                        t.wait_ge(g_wo.sem, gld_wo)
                    for og in range(4):
                        for it in range(NDT):
                            inst = t.matmul(ps[:, og * 512:(og + 1) * 512],
                                            ygt[:, it * CL:(it + 1) * CL],
                                            woT_sb[:, it, og * 512:(og + 1) * 512],
                                            start=(it == 0), stop=(it == NDT - 1))
                    c_pe.inc(inst, 1)
                wo_pe = c_pe.n

                @block.vector
                def _(v, ci=ci, wo_pe=wo_pe):
                    v = _CH(v, c_ve)
                    v.wait_ge(c_pe.sem, wo_pe)
                    v.wait_ge(l_uc.sem, scl_uc[ci])
                    if ci >= 1:
                        v.wait_ge(c_ac.sem, ln_ac_hist[ci - 1])
                    inst = v.tensor_tensor(r1s[:, :], ps[:, 0:D], uc[:, :], ALU.add)
                wo_r1_ve = c_ve.n

                if ci + 1 < min(NCH, S2_CHUNKS):
                    @block.scalar
                    def _(s, ci=ci, wo_r1_ve=wo_r1_ve):
                        s = _CH(s, c_ac)
                        s.wait_ge(c_ve.sem, wo_r1_ve)
                        l_uc.inc(s.dma_start(out=uc[:, :],
                                             in_=d_ur[(ci + 1) * CL:(ci + 2) * CL, :]), 16)
                        scl_uc[ci + 1] = l_uc.n

                ln_pend = (ci, wo_r1_ve)

                chunk_ve[ci] = c_ve.n
                chunk_ac[ci] = c_ac.n

                if ci + 1 < min(NCH, S2_CHUNKS):
                    @block.scalar
                    def _(s, ci=ci, carry2_ve=carry2_ve):
                        s = _CH(s, c_ac)
                        s.wait_ge(c_ve.sem, carry2_ve)
                        l_stc.inc(s.dma_start(out=stc[:, :],
                                              in_=d_stsp[:, (ci + 1) * 1024:(ci + 2) * 1024]), 16)
                        scl_st[ci + 1] = l_stc.n

            if x1tr_pend is not None:
                consume_x1t(wo_r1_ve)
                x1tr_pend = None
            if ln_pend is not None:
                lpci, p_wo_r1 = ln_pend
                x1_ve, _lnac = emit_ln(lpci, p_wo_r1, x1_ve)
                ln_ac_hist[lpci] = _lnac
                x1tr_pend = (lpci, x1_ve)
                ln_pend = None
            if x1tr_pend is not None:
                consume_x1t(wo_r1_ve)
                x1tr_pend = None

            s2_end = snap()

        if STAGE_LIMIT < 4:
            _finalize(block, ALLC)
            st2.close()
            return nc

        st2.close()
        st3a = contextlib.ExitStack()
        x1T = st3a.enter_context(nc.sbuf_tensor("x1T", [P, NDT, T], BF16))
        wg0 = st3a.enter_context(nc.sbuf_tensor("wg0", [P, NDT, P], BF16))
        wg1 = st3a.enter_context(nc.sbuf_tensor("wg1", [P, NDT, P], BF16))
        wu0 = st3a.enter_context(nc.sbuf_tensor("wu0", [P, NDT, P], BF16))
        wu1 = st3a.enter_context(nc.sbuf_tensor("wu1", [P, NDT, P], BF16))
        sgl = st3a.enter_context(nc.sbuf_tensor("sgl", [P, T], BF16))
        ght0 = st3a.enter_context(nc.sbuf_tensor("ght0", [P, T], BF16))
        ght1 = st3a.enter_context(nc.sbuf_tensor("ght1", [P, T], BF16))
        if True:
            wg = [wg0, wg1]
            wu = [wu0, wu1]
            ght = [ght0, ght1]

            @block.sync
            def _(sy):
                wait_all(sy, s2_end)
                y_x1t.inc(sy.dma_start(
                    out=x1T[:, :, :].rearrange("p dt (ci t) -> p dt ci t", t=CL),
                    in_=d_x1t.rearrange("(ci p) (dt t) -> p dt ci t", p=P, t=CL)), 16)
            syn_x1t = y_x1t.n

            wqg_src = d_wqg.rearrange("(dt p) h -> p dt h", p=P)
            wqu_src = d_wqu.rearrange("(dt p) h -> p dt h", p=P)

            mm_pe_m = {}
            ev_ve_m = {}
            sp_scl_m = {}
            gld_m = {}
            for ht in range(NHT):
                @block.gpsimd
                def _(g, ht=ht):
                    g = _CH(g, c_gp)
                    if ht == 0:
                        wait_all(g, s2_end)
                    if ht >= 2:
                        g.wait_ge(c_pe.sem, mm_pe_m[ht - 2])
                    g.memset(wg[ht % 2][:, :, :], -128.0)
                    g.memset(wu[ht % 2][:, :, :], -128.0)
                    g.wait_ge(c_gp.sem, c_gp.n)
                    g_wg[ht % 2].inc(g.dma_start(out=wg[ht % 2][:, :, :],
                                          in_=wqg_src[:, :, ht * P:(ht + 1) * P],
                                          accum_op=ALU.add), 16)
                    g_wu[ht % 2].inc(g.dma_start(out=wu[ht % 2][:, :, :],
                                          in_=wqu_src[:, :, ht * P:(ht + 1) * P],
                                          accum_op=ALU.add), 16)
                gld_m[ht] = (g_wg[ht % 2].mark(), g_wu[ht % 2].mark())

                @block.tensor
                def _(t, ht=ht):
                    if ht == 0:
                        t.wait_ge(y_x1t.sem, syn_x1t)
                    wait_mark(t, gld_m[ht][0])
                    wait_mark(t, gld_m[ht][1])
                    if ht >= 1:
                        t.wait_ge(c_ve.sem, ev_ve_m[ht - 1])
                    for tg in range(4):
                        for dti in range(NDT):
                            t.matmul(ps[:, tg * 512:(tg + 1) * 512],
                                     wg[ht % 2][:, dti, :],
                                     x1T[:, dti, tg * 512:(tg + 1) * 512],
                                     start=(dti == 0), stop=(dti == NDT - 1))
                        for dti in range(NDT):
                            inst = t.matmul(ps[:, 2048 + tg * 512:2048 + (tg + 1) * 512],
                                            wu[ht % 2][:, dti, :],
                                            x1T[:, dti, tg * 512:(tg + 1) * 512],
                                            start=(dti == 0), stop=(dti == NDT - 1))
                        c_pe.inc(inst, 1)
                mm_pe_m[ht] = c_pe.n

                @block.scalar
                def _(s, ht=ht):
                    s = _CH(s, c_ac)
                    for tg in range(4):
                        s.wait_ge(c_pe.sem, mm_pe_m[ht] - 3 + tg)
                        inst = s.activation(sgl[:, tg * 512:(tg + 1) * 512],
                                            ps[:, tg * 512:(tg + 1) * 512], AF.Silu,
                                            scale=sg_sb[:, ht:ht + 1])
                sil_ac = c_ac.n

                @block.vector
                def _(v, ht=ht, sil_ac=sil_ac):
                    v = _CH(v, c_ve)
                    if ht >= 2:
                        wait_mark(v, sp_scl_m[ht - 2])
                    for tg in range(4):
                        v.wait_ge(c_ac.sem, sil_ac - 3 + tg)
                        inst = v.scalar_tensor_tensor(
                            ght[ht % 2][:, tg * 512:(tg + 1) * 512],
                            ps[:, 2048 + tg * 512:2048 + (tg + 1) * 512],
                            su_sb[:, ht:ht + 1],
                            sgl[:, tg * 512:(tg + 1) * 512],
                            ALU.mult, ALU.mult)
                ev_ve_m[ht] = c_ve.n

                @block.scalar
                def _(s, ht=ht):
                    s = _CH(s, c_ac)
                    s.wait_ge(c_ve.sem, ev_ve_m[ht])
                    l_gh[ht % 2].inc(s.dma_start(out=d_gh[ht * P:(ht + 1) * P, :],
                                          in_=ght[ht % 2][:, :]), 16)
                sp_scl_m[ht] = l_gh[ht % 2].mark()

            s3a_end = snap()

        if STAGE_LIMIT < 5:
            _finalize(block, ALLC)
            st3a.close()
            return nc

        st3a.close()
        st3b = contextlib.ExitStack()
        wdh = st3b.enter_context(nc.sbuf_tensor("wdh", [P, NHT, 1024], BF16))
        ghc0 = st3b.enter_context(nc.sbuf_tensor("ghc0", [P, NHT, P], BF16))
        ghc1 = st3b.enter_context(nc.sbuf_tensor("ghc1", [P, NHT, P], BF16))
        sdb = st3b.enter_context(nc.sbuf_tensor("sdb", [P, D], F32))
        dnv = st3b.enter_context(nc.sbuf_tensor("dnv", [P, 1024], F32))
        x1ca = st3b.enter_context(nc.sbuf_tensor("x1ca", [P, D], F32))
        x1cb = st3b.enter_context(nc.sbuf_tensor("x1cb", [P, D], F32))
        dn0ca = st3b.enter_context(nc.sbuf_tensor("dn0ca", [P, 1024], F32))
        dn0cb = st3b.enter_context(nc.sbuf_tensor("dn0cb", [P, 1024], F32))
        r2s = st3b.enter_context(nc.sbuf_tensor("r2s", [P, D], F32))
        scr3 = st3b.enter_context(nc.sbuf_tensor("scr3", [P, D], F32))
        outsa = st3b.enter_context(nc.sbuf_tensor("outsa", [P, D], F32))
        outsb = st3b.enter_context(nc.sbuf_tensor("outsb", [P, D], F32))
        sdvs = st3b.enter_context(nc.sbuf_tensor("sdvs", [1, D], F32))
        if True:
            ghc = [ghc0, ghc1]
            x1c = [x1ca, x1cb]
            dn0c = [dn0ca, dn0cb]
            outs = [outsa, outsb]
            y_xd2 = [mkc("y_xd20"), mkc("y_xd21")]
            y_o2 = [mkc("y_o20"), mkc("y_o21")]
            for _c in y_xd2 + y_o2:
                ALLC.append(_c)
            wqd_src = d_wqd.rearrange("(ht p) o -> p ht o", p=P)
            gh_src = d_gh.rearrange("(ht p) t -> p ht t", p=P)

            @block.sync
            def _(sy):
                wait_all(sy, s3a_end)
                y_sdv.inc(sy.dma_start(out=sdvs[0:1, :], in_=d_sdv), 16)
            syn_sdv = y_sdv.n

            @block.tensor
            def _(t):
                t.wait_ge(y_sdv.sem, syn_sdv)
                wait_all(t, s3a_end)
                for og in range(4):
                    inst = t.matmul(ps[:, 2048 + og * 512:2048 + (og + 1) * 512],
                                    on_sb[0:1, :], sdvs[0:1, og * 512:(og + 1) * 512],
                                    start=True, stop=True)
                c_pe.inc(inst, 1)
            pe_sdb = c_pe.n

            @block.vector
            def _(v):
                v = _CH(v, c_ve)
                v.wait_ge(c_pe.sem, pe_sdb)
                wait_all(v, s3a_end)
                inst = v.tensor_copy(sdb[:, :], ps[:, 2048:4096])
            ve_sdb = c_ve.n

            mk = {}
            for oh in range(2):
                @block.gpsimd
                def _(g, oh=oh):
                    g = _CH(g, c_gp)
                    if oh == 0:
                        wait_all(g, s3a_end)
                    else:
                        g.wait_ge(c_pe.sem, mk[("half_pe", 0)])
                    negc_ap = AP(d_negc.tensor, d_negc.offset, [[0, P], [1, NHT * 1024]])
                    g_wd.inc(g.dma_start(out=wdh[:, :, :].rearrange("p h o -> p (h o)"),
                                         in_=negc_ap), 16)
                gld_f = g_wd.n

                @block.gpsimd
                def _(g, oh=oh, gld_f=gld_f):
                    g = _CH(g, c_gp)
                    g.wait_ge(g_wd.sem, gld_f)
                    g_wd.inc(g.dma_start(out=wdh[:, :, :],
                                          in_=wqd_src[:, :, oh * 1024:(oh + 1) * 1024],
                                          accum_op=ALU.add), 16)
                gld_wd = g_wd.n

                for tt in range(16):
                    pfb = tt % 2
                    ts0 = tt * CL
                    base = pfb * 1024

                    @block.sync
                    def _(sy, oh=oh, tt=tt, pfb=pfb, ts0=ts0):
                        if oh == 0 and tt == 0:
                            wait_all(sy, s3a_end)
                        gi = oh * 16 + tt
                        if gi >= 2:
                            po, ptt = divmod(gi - 2, 16)
                            sy.wait_ge(c_pe.sem, mk[("pe", po, ptt)])
                        y_gc[pfb].inc(sy.dma_start(out=ghc[pfb][:, :, :],
                                               in_=gh_src[:, :, ts0:ts0 + CL]), 16)
                        mk[("syn", oh, tt)] = y_gc[pfb].mark()
                        if oh == 1:
                            if tt >= 2:
                                sy.wait_ge(c_ve.sem, mk[("r2", tt - 2)])
                            if tt == 0:
                                sy.wait_ge(l_dn.sem, l_dn.n)
                                sy.wait_ge(l_x1.sem, l_x1.n)
                            y_xd2[pfb].inc(sy.dma_start(out=x1c[pfb][:, :],
                                                        in_=d_x1[ts0:ts0 + CL, :]), 16)
                            y_xd2[pfb].inc(sy.dma_start(out=dn0c[pfb][:, :],
                                                        in_=d_dn0[ts0:ts0 + CL, :]), 16)
                            mk[("xd", tt)] = y_xd2[pfb].mark()

                    @block.tensor
                    def _(t, oh=oh, tt=tt, pfb=pfb, base=base, gld_wd=gld_wd):
                        t.wait_ge(g_wd.sem, gld_wd)
                        wait_mark(t, mk[("syn", oh, tt)])
                        gi = oh * 16 + tt
                        if gi >= 2:
                            po, ptt = divmod(gi - 2, 16)
                            wait_mark(t, mk[("evd", po, ptt)])
                        for ht in range(NHT):
                            for og in range(2):
                                inst = t.matmul(ps[:, base + og * 512:base + (og + 1) * 512],
                                                ghc[pfb][:, ht, :],
                                                wdh[:, ht, og * 512:(og + 1) * 512],
                                                start=(ht == 0), stop=(ht == NHT - 1))
                        c_pe.inc(inst, 1)
                        mk[("pe", oh, tt)] = c_pe.n

                    if oh == 0:
                        @block.vector
                        def _(v, tt=tt, base=base):
                            v = _CH(v, c_ve)
                            v.wait_ge(c_pe.sem, mk[("pe", 0, tt)])
                            v.wait_ge(c_ve.sem, ve_sdb)
                            if tt >= 1:
                                v.wait_ge(l_dn.sem, mk[("dnscl", tt - 1)])
                            inst = v.tensor_tensor(dnv[:, :], ps[:, base:base + 1024],
                                                   sdb[:, 0:1024], ALU.mult)
                            mk[("evd", 0, tt)] = (c_ve, c_ve.n)

                        @block.scalar
                        def _(s, tt=tt, ts0=ts0):
                            s = _CH(s, c_ac)
                            wait_mark(s, mk[("evd", 0, tt)])
                            l_dn.inc(s.dma_start(out=d_dn0[ts0:ts0 + CL, :], in_=dnv[:, :]), 16)
                            mk[("dnscl", tt)] = l_dn.n
                    else:
                        @block.vector
                        def _(v, tt=tt, base=base, pfb=pfb):
                            v = _CH(v, c_ve)
                            v.wait_ge(c_pe.sem, mk[("pe", 1, tt)])
                            wait_mark(v, mk[("xd", tt)])
                            v.tensor_tensor(r2s[:, 1024:2048], ps[:, base:base + 1024],
                                            sdb[:, 1024:2048], ALU.mult)
                            v.tensor_tensor(r2s[:, 1024:2048], r2s[:, 1024:2048],
                                            x1c[pfb][:, 1024:2048], ALU.add)
                            inst = v.tensor_tensor(r2s[:, 0:1024], dn0c[pfb][:, :],
                                                   x1c[pfb][:, 0:1024], ALU.add)
                            mk[("evd", 1, tt)] = (c_ve, c_ve.n)
                            mk[("r2", tt)] = c_ve.n

                        @block.scalar
                        def _(s, tt=tt):
                            s = _CH(s, c_ac)
                            s.wait_ge(c_ve.sem, mk[("r2", tt)])
                            s.activation(scr3[:, :], r2s[:, :], AF.Square, scale=RS,
                                         accum_out=stat[:, 0:1])
                            inst = s.activation(scr3[:, :], r2s[:, :], AF.Identity, scale=RT,
                                                accum_out=stat[:, 1:2])
                            mk[("ac", tt)] = c_ac.n

                        @block.vector
                        def _(v, tt=tt):
                            v = _CH(v, c_ve)
                            v.wait_ge(c_ac.sem, mk[("ac", tt)])
                            inst = v.scalar_tensor_tensor(stat[:, 2:3], stat[:, 1:2],
                                                          stat[:, 1:2], stat[:, 0:1],
                                                          ALU.mult, ALU.subtract)
                            mk[("nv", tt)] = c_ve.n

                        @block.scalar
                        def _(s, tt=tt):
                            s = _CH(s, c_ac)
                            s.wait_ge(c_ve.sem, mk[("nv", tt)])
                            s.activation(stat[:, 3:4], stat[:, 2:3], AF.Ln, scale=-1.0, bias=EPS)
                            inst = s.activation(stat[:, 4:5], stat[:, 3:4], AF.Exp, scale=-0.5)
                            mk[("rstd", tt)] = c_ac.n

                        @block.vector
                        def _(v, tt=tt, pfb=pfb):
                            v = _CH(v, c_ve)
                            v.wait_ge(c_ac.sem, mk[("rstd", tt)])
                            if tt >= 2:
                                wait_mark(v, mk[("odma", tt - 2)])
                            inst = v.tensor_scalar(outs[pfb][:, :], r2s[:, :], stat[:, 1:2],
                                                   stat[:, 4:5], ALU.subtract, ALU.mult)
                            mk[("out", tt)] = c_ve.n

                        @block.gpsimd
                        def _(g, tt=tt, ts0=ts0, pfb=pfb):
                            g = _CH(g, c_gp)
                            g.wait_ge(c_ve.sem, mk[("out", tt)])
                            y_o2[pfb].inc(g.dma_start(out=d_out[ts0:ts0 + CL, :],
                                                      in_=outs[pfb][:, :]), 16)
                            mk[("odma", tt)] = y_o2[pfb].mark()

                mk[("half_pe", oh)] = c_pe.n

            @block.sync
            def _(sy):
                sy.wait_ge(y_o2[0].sem, y_o2[0].n)
                sy.wait_ge(y_o2[1].sem, y_o2[1].n)
                sy.wait_ge(l_dn.sem, l_dn.n)
        st3b.close()

    return nc


def kernel(L0_attn_u, L0_attn_W_in, L0_attn_conv_w, L0_attn_conv_b, L0_attn_D,
           L0_attn_z_bias, L0_attn_W_out, L0_ffn_wq_gate, L0_ffn_s_gate,
           L0_ffn_wq_up, L0_ffn_s_up, L0_ffn_wq_down, L0_ffn_s_down):
    u = np.asarray(L0_attn_u, dtype=np.float32)
    W_in = np.asarray(L0_attn_W_in, dtype=np.float32)
    conv_w = np.asarray(L0_attn_conv_w, dtype=np.float32)
    conv_b = np.asarray(L0_attn_conv_b, dtype=np.float32)
    Dv = np.asarray(L0_attn_D, dtype=np.float32)
    zb = np.asarray(L0_attn_z_bias, dtype=np.float32)
    W_out = np.asarray(L0_attn_W_out, dtype=np.float32)
    wqg = np.ascontiguousarray(np.asarray(L0_ffn_wq_gate, dtype=np.int32))
    sgv = np.asarray(L0_ffn_s_gate, dtype=np.float32)
    wqu = np.ascontiguousarray(np.asarray(L0_ffn_wq_up, dtype=np.int32))
    suv = np.asarray(L0_ffn_s_up, dtype=np.float32)
    wqd = np.ascontiguousarray(np.asarray(L0_ffn_wq_down, dtype=np.int32))
    sdw = np.asarray(L0_ffn_s_down, dtype=np.float32)

    def lowbyte(a):
        return a.view(np.uint8).reshape(a.shape[0], a.shape[1], 4)[:, :, 0]

    wiT = np.ascontiguousarray(W_in.T)
    woT = np.ascontiguousarray(W_out.T)
    wqgT = np.ascontiguousarray(lowbyte(wqg).T)
    wquT = np.ascontiguousarray(lowbyte(wqu).T)
    wqdT = np.ascontiguousarray(lowbyte(wqd).T)
    cw = np.ascontiguousarray(conv_w.reshape(48, P, 4).transpose(1, 0, 2))
    cb = np.ascontiguousarray(conv_b.reshape(48, P).T)
    dpk = np.ascontiguousarray(np.repeat(Dv, 64).reshape(16, P).T)
    zbp = np.ascontiguousarray(zb.reshape(16, P).T)
    sgp = np.ascontiguousarray(sgv.reshape(NHT, P).T)
    sup = np.ascontiguousarray(suv.reshape(NHT, P).T)
    sdv2 = np.ascontiguousarray(sdw.reshape(1, D))
    idm = np.eye(P, dtype=np.float32)
    import ml_dtypes
    idb = np.eye(P, dtype=ml_dtypes.bfloat16)
    k_ = np.arange(P)
    lnm = np.ascontiguousarray(-(k_[:, None] <= k_[None, :]).astype(np.float32))
    ngm = np.ascontiguousarray(
        np.where(k_[None, :] >= k_[:, None], 0.0, -60000.0).astype(np.float32))
    onr = np.ones((1, P), dtype=np.float32)
    negc = np.full((1, NHT * 1024), -128.0, dtype=ml_dtypes.bfloat16)

    in_maps = []
    for c in range(8):
        b, j = c // 2, c % 2
        t0 = j * T
        useg = np.ascontiguousarray(u[b, t0:t0 + T, :])
        if j == 0:
            uh = np.concatenate([np.zeros((3, D), np.float32), u[b, 0:T, :]], axis=0)
        else:
            uh = u[b, t0 - 3:t0 + T, :]
        uT = np.ascontiguousarray(uh.T)
        msk = np.full((P, 1), 1.0 if j == 1 else 0.0, np.float32)
        in_maps.append({
            "uT": uT, "ur": useg,
            "wiT": wiT, "cw": cw, "cb": cb, "dpk": dpk, "zbp": zbp,
            "woT": woT, "wqg": wqgT, "wqu": wquT, "wqd": wqdT,
            "sg": sgp, "su": sup, "sdv": sdv2, "msk": msk,
            "idm": idm, "idb": idb, "lnm": lnm, "ngm": ngm, "onr": onr,
            "negc": negc,
        })

    nc = build_nc()
    res = run_bass_kernel_spmd(nc, in_maps, core_ids=list(range(8)))
    out = np.empty((4, 4096, D), np.float32)
    for c in range(8):
        b, j = c // 2, c % 2
        out[b, j * T:(j + 1) * T, :] = res.results[c]["out"]
    return out
